# version 36
# baseline (speedup 1.0000x reference)
import contextlib
import numpy as np
import ml_dtypes

import concourse.bass as bass
import concourse.mybir as mybir
from concourse.bass_utils import run_bass_kernel_spmd

F32 = mybir.dt.float32
BF16 = mybir.dt.bfloat16
AF = mybir.ActivationFunctionType
ALU = mybir.AluOpType
AX = mybir.AxisListType.X

DM = 1024
DH = 1024
NS = 16
L = 4096
TC = 512
NT = 8
DFF = 4096
EPS_LN = 1e-5
EPS_RMS = float(np.finfo(np.float32).eps)
SP_CLAMP = 30.0
GROUPS = [[0, 1], [2, 3], [4, 5], [6, 7]]

ENGS = ("sync", "tensor", "vector", "scalar", "gpsimd")


class Em:
    def __init__(self, nc):
        self.nc = nc
        self.streams = {e: [] for e in ENGS}
        self.sems = {}
        self.cnt = {}
        self.eng_of = {}
        self.last_wait = {e: {} for e in ENGS}
        self.drained = {}
        self.es = contextlib.ExitStack()

    def chan(self, name, engine):
        if name not in self.sems:
            self.sems[name] = self.es.enter_context(self.nc.semaphore(name))
            self.cnt[name] = 0
            self.eng_of[name] = engine
        return self.sems[name]

    def op(self, engine, build, waits=(), inc=None, inc_by=1):
        if inc is not None:
            self.chan(inc, engine)
            self.cnt[inc] += inc_by
        lw = self.last_wait[engine]
        need_drain = False
        real = []
        for w in waits:
            if w is None:
                continue
            nm, val = w
            if val <= 0:
                continue
            if self.eng_of.get(nm) == engine and nm == inc:
                val = min(val, self.cnt[nm] - inc_by)
                if val <= 0:
                    continue
            if lw.get(nm, -1) >= val:
                continue
            lw[nm] = val
            real.append((nm, val))
        sems = self.sems

        step = 1 if (inc in ("pe", "ve", "sc", "gp") or
                     (inc is not None and inc.startswith("c"))) else 16

        def fn(eng, real=tuple(real), inc=inc, inc_by=inc_by, build=build,
               step=step, need_drain=need_drain):
            if need_drain:
                eng.drain()
            for nm, val in real:
                eng.wait_ge(sems[nm], val)
            ins = build(eng)
            if inc is not None and ins is not None:
                ins.then_inc(sems[inc], step * inc_by)
        self.streams[engine].append(fn)

    def v(self, name):
        return self.cnt.get(name, 0)

    def finalize(self, block):
        for engine in ENGS:
            stream = self.streams[engine]

            def run(eng, stream=stream):
                for fn in stream:
                    fn(eng)
            getattr(block, engine)(run)


def build_nc(debug=False):
    nc = bass.Bass("TRN2", target_bir_lowering=False)
    em = Em(nc)
    es = em.es

    def din(name, shape, dtype=F32):
        return nc.dram_tensor(name, list(shape), dtype, kind="ExternalInput").ap()

    x_g = din("x_g", (L, DM))
    x_fin = din("x_fin", (4 * TC, DM))
    wi_ssm = din("wi_ssm", (DM, DH))
    wi_gate = din("wi_gate", (DM, DH))
    w_x_h = din("w_x_h", (DH, 96))
    w_dt_h = din("w_dt_h", (64, DH))
    w_out_h = din("w_out_h", (DH, DM))
    diag = din("diag", (32, 128, 128))
    cb_col = din("cb_col", (128, 8))
    bdt_col = din("bdt_col", (128, 8))
    D_col = din("D_col", (128, 8))
    nw_col = din("nw_col", (128, 8))
    A_lay = din("A_lay", (128, 128))
    w1_a = din("w1_a", (DM, DFF), BF16); w1_b = din("w1_b", (DM, DFF), BF16)
    w2_a = din("w2_a", (DFF, DM), BF16); w2_b = din("w2_b", (DFF, DM), BF16)
    b1_cols = din("b1_cols", (128, 64))
    b2_cols = din("b2_cols", (128, 16))
    identf = din("identf", (128, 128))
    identb = din("identb", (128, 128), BF16)
    sel16 = din("sel16", (16, 16 * 128), BF16)

    out = nc.dram_tensor("o", [4 * TC, DM], F32, kind="ExternalOutput").ap()

    dbg = {}
    if debug:
        def dout(name, shape, dtype=F32):
            dbg[name] = nc.dram_tensor(name, list(shape), dtype, kind="ExternalOutput").ap()
        dout("d_xnfm", (8, 128, TC), BF16)
        dout("d_xn2", (8, 128, TC), BF16)
        dout("d_xact", (NT, 8, 128, TC), BF16)
        dout("d_xdbl", (NT, 96, TC), BF16)
        dout("d_delta", (NT, 8, 128, TC), BF16)
        dout("d_y", (NT, 8, 128, TC), BF16)
        dout("d_xf", (NT, 8, 128, TC), BF16)
        dout("d_h0", (NT, 8, 128, TC), BF16)
        dout("d_hff", (8, 128, 256), BF16)
        dout("d_xfop", (NT, 8, 128, TC), BF16)
        dout("d_rso", (4, DM, TC), BF16)

    def dint(name, shape, dtype):
        return nc.dram_tensor(name, list(shape), dtype, kind="Internal").ap()

    wigcm = dint("wigcm", (8, 128, 8, 128), BF16)
    wiscm = dint("wiscm", (8, 128, 8, 128), BF16)
    woutcm = dint("woutcm", (8, 128, 8, 128), BF16)
    w1cm = [dint(f"w1cm{e}", (32, 128, 8, 128), BF16) for e in range(2)]
    w2cm = [dint(f"w2cm{e}", (8, 128, 32, 128), BF16) for e in range(2)]
    ar_i = [dint(f"ar_i{t}", (96, TC), BF16) for t in range(NT)]
    ar_o = [dint(f"ar_o{t}", (96, TC), BF16) for t in range(NT)]
    rs_i = [dint(f"rs_i{i}", (2, DM, TC), BF16) for i in range(4)]
    rs_o = [dint(f"rs_o{i}", (DM, TC), BF16) for i in range(4)]

    def sb(name, cols, dtype=F32):
        return es.enter_context(nc.sbuf_tensor(name, [128, cols], dtype))

    def psum(name, cols, dtype=F32):
        return es.enter_context(nc.psum_tensor(name, [128, cols], dtype))

    s_nw = sb("s_nw", 8); s_cb = sb("s_cb", 8); s_bdt = sb("s_bdt", 8)
    s_D = sb("s_D", 8); s_An = sb("s_An", 128)
    s_idf = sb("s_idf", 128); s_idb = sb("s_idb", 128, BF16)
    s_b1 = sb("s_b1", 64); s_b2 = sb("s_b2", 16)
    s_sel = sb("s_sel", 4 * 128, BF16)
    s_eps = sb("s_eps", 2)
    s_dg = sb("s_dg", 32 * 128, BF16)
    s_wx = sb("s_wx", 8 * 96, BF16)
    s_wdt = sb("s_wdt", DH, BF16)
    s_wi = sb("s_wi", 2 * 1024, BF16)

    s_xg = sb("s_xg", DM)
    s_st = sb("s_st", 32)
    s_rms = sb("s_rms", 16)
    s_xnt = sb("s_xnt", DM, BF16)
    s_rT = sb("s_rT", 128, BF16)
    s_rbc = sb("s_rbc", TC, BF16)
    s_xnfm = sb("s_xnfm", 8 * TC, BF16)
    s_xn2 = sb("s_xn2", 2 * 8 * TC, BF16)
    s_xsc = sb("s_xsc", 2 * (TC + 3), BF16)
    s_hist = sb("s_hist", 24, BF16)
    s_xact = sb("s_xact", 2 * 8 * TC, BF16)
    s_xdm = sb("s_xdm", TC, BF16)

    s_xdf = sb("s_xdf", TC, BF16)
    s_sfs = sb("s_sfs", TC)
    s_del = sb("s_del", 2 * 8 * TC, BF16)
    s_du = sb("s_du", 8 * TC, BF16)
    s_y = sb("s_y", 8 * TC, BF16)
    s_dA = sb("s_dA", 3 * TC, BF16)
    s_X = sb("s_X", 3 * TC, BF16)
    s_pr = s_X
    s_h = sb("s_h", 8 * TC, BF16)
    s_Bb = sb("s_Bb", 4 * TC, BF16)
    s_Cb = sb("s_Cb", 4 * TC, BF16)
    s_car = sb("s_car", 128)
    s_xf = sb("s_xf", 8 * TC, BF16)
    s_opb = sb("s_opb", TC, BF16)
    s_wgm = sb("s_wgm", 2 * 1024, BF16)
    s_wom = sb("s_wom", 2 * 1024, BF16)

    s_rs = sb("s_rs", 2 * TC, BF16)
    s_xfi = sb("s_xfi", 2 * DM)
    s_fst = sb("s_fst", 32)
    s_xnf = sb("s_xnf", DM)
    s_xnff = sb("s_xnff", 8 * TC, BF16)
    s_hff = sb("s_hff", 32 * 256, BF16)
    s_w1m = sb("s_w1m", 4 * 1024, BF16)
    s_w2d = sb("s_w2d", 2 * 2048, BF16)
    s_omf = sb("s_omf", 4 * 256)
    s_otk = sb("s_otk", TC)

    p_f0 = psum("p_f0", TC)
    p_f1 = psum("p_f1", TC)
    p_ft = psum("p_ft", 1024, BF16)
    p_db = psum("p_db", TC)
    p_do = psum("p_do", TC)
    p_bc = psum("p_bc", TC)
    p_n0 = psum("p_n0", TC)
    p_n1 = psum("p_n1", TC)

    blk = es.enter_context(nc.Block())
    OP = em.op
    V = em.v

    free = {k: ("ve", 0) for k in ("f0", "f1", "ft", "db", "do", "bc", "n0", "n1")}
    bf = {}
    PUMP = [lambda k=1: None]
    hi = []
    lo = []
    parked = {}
    rs_issued = set()

    for dst, src in ((s_nw, nw_col), (s_cb, cb_col), (s_bdt, bdt_col),
                     (s_D, D_col), (s_idf, identf), (s_idb, identb),
                     (s_b1, b1_cols), (s_b2, b2_cols)):
        OP("sync", lambda e, d=dst, s=src: e.dma_start(out=d[:], in_=s), inc="ini")
    OP("sync", lambda e: e.dma_start(out=s_An[:], in_=A_lay), inc="ini")
    OP("sync", lambda e: e.dma_start(out=s_sel[0:4, :], in_=sel16[0:4, 0:512]), inc="ini")
    ini_n = 16 * V("ini")

    for m in range(8):
        OP("gpsimd", lambda e, m=m: e.dma_start(
            out=wiscm[m],
            in_=wi_ssm.rearrange("(k p) (m c) -> m p k c", p=128, c=128)[m]),
           inc="gdw")
    OP("gpsimd", lambda e: e.dma_start(
        out=s_wx[:].rearrange("p (k c) -> p k c", c=96),
        in_=w_x_h.rearrange("(k p) c -> p k c", p=128)), inc="gds")
    OP("gpsimd", lambda e: e.dma_start(out=s_wdt[0:64, :], in_=w_dt_h), inc="gds")
    OP("gpsimd", lambda e: e.dma_start(
        out=s_dg[:].rearrange("p (j c) -> p j c", c=128),
        in_=diag.rearrange("j p c -> p j c")), inc="gds")
    gds_n = 16 * V("gds")
    for m in range(8):
        OP("gpsimd", lambda e, m=m: e.dma_start(
            out=wigcm[m],
            in_=wi_gate.rearrange("(k p) (m c) -> m p k c", p=128, c=128)[m]), inc="gdw")
        OP("gpsimd", lambda e, m=m: e.dma_start(
            out=woutcm[m],
            in_=w_out_h.rearrange("(k p) (m c) -> m p k c", p=128, c=128)[m]), inc="gdw")
    for ei, w1_, w2_ in ((0, w1_a, w2_a), (1, w1_b, w2_b)):
        for m in range(32):
            OP("sync" if m % 2 == 0 else "scalar",
               lambda e, ei=ei, m=m, w1_=w1_: e.dma_start(
                   out=w1cm[ei][m],
                   in_=w1_.rearrange("(k p) (m c) -> m p k c", p=128, c=128)[m]),
               inc="gwf%d" % ei)
        for m in range(8):
            OP("sync" if m % 2 == 0 else "scalar",
               lambda e, ei=ei, m=m, w2_=w2_: e.dma_start(
                   out=w2cm[ei][m],
                   in_=w2_.rearrange("(k p) (m c) -> m p k c", p=128, c=128)[m]),
               inc="gwf%d" % ei)
    gdw_n = 16 * V("gdw")
    gwf_n = (16 * V("gwf0"), 16 * V("gwf1"))

    OP("scalar", lambda e: e.activation(s_An[:], s_An[:], AF.Exp, bias=0.0, scale=1.0),
       waits=[("ini", ini_n)], inc="sc")
    OP("vector", lambda e: e.tensor_scalar(s_An[:], s_An[:], -1.0, None, ALU.mult),
       waits=[("sc", V("sc"))], inc="ve")
    aneg_ve = V("ve")
    OP("gpsimd", lambda e: e.memset(s_hist[:], 0.0), inc="gp")
    hist0_gp = V("gp")
    OP("vector", lambda e: e.memset(s_eps[:, 0:1], EPS_LN), inc="ve")
    OP("vector", lambda e: e.memset(s_eps[:, 1:2], EPS_RMS), inc="ve")
    eps_ve = V("ve")

    W_xn2 = [0] * NT
    W_xact = [0] * NT
    W_ar = [0] * NT
    W_rs = [0] * 4
    W_sdone = [dict(pe=0, ve=0, sc=0, gp=0) for _ in range(NT)]
    W_gatepe = [0] * NT
    W_stats = [0] * NT
    W_fstats = [0] * 4
    W_statsc = {}
    W_fstatsc = {}
    W_carn = {}
    W_xactfree = {}
    W_dugp = {}
    W_nloopve = {}
    W_xfve = {}

    def front_gen(t):
        xn2_o = (t % 2) * 8 * TC
        xact_o = (t % 2) * 8 * TC
        sd_gp = W_xactfree.get(t - 2, 0)

        ev = [None]
        for j in range(4):
            r0 = t * TC + j * 128
            OP("sync", lambda e, r0=r0: e.dma_start(out=s_xg[:], in_=x_g[r0:r0 + 128, :]),
               waits=[("ve", bf.get("xg_ve", 0)), ("sc", bf.get("xg_sc", 0))], inc="xg")
            ld = 16 * V("xg")
            OP("vector", lambda e: e.bn_stats(s_st[:, 0:6], s_xg[:, 0:512]),
               waits=[("xg", ld), ("ini", ini_n),
                      ("sc", W_statsc.get(t - 1, 0)), ("ve", V("ve"))], inc="ve")
            OP("vector", lambda e: e.bn_stats(s_st[:, 6:12], s_xg[:, 512:1024]),
               waits=[("ve", V("ve"))], inc="ve")
            OP("vector", lambda e, j=j: e.bn_aggr(
                s_st[:, 16 + 2 * j:18 + 2 * j], s_st[:, 0:12]),
               waits=[("ve", V("ve"))], inc="ve")
            OP("scalar", lambda e, j=j: e.activation(
                s_st[:, 24 + j:25 + j], s_st[:, 17 + 2 * j:18 + 2 * j],
                AF.Ln, bias=s_eps[:, 0:1], scale=1.0),
               waits=[("ve", V("ve")), ("sc", V("sc"))], inc="sc")
            OP("scalar", lambda e, j=j: e.activation(
                s_st[:, 28 + j:29 + j], s_st[:, 24 + j:25 + j],
                AF.Exp, bias=0.0, scale=-0.5),
               waits=[("sc", V("sc"))], inc="sc")
            rstd_sc = V("sc")
            bf["xg_sc"] = rstd_sc
            OP("vector", lambda e, j=j: e.tensor_tensor(
                s_rms[:, 4 + j:5 + j], s_st[:, 28 + j:29 + j],
                s_st[:, 28 + j:29 + j], ALU.mult),
               waits=[("sc", rstd_sc), ("ve", V("ve"))], inc="ve")
            OP("vector", lambda e, j=j: e.tensor_tensor(
                s_rms[:, j:j + 1], s_st[:, 17 + 2 * j:18 + 2 * j],
                s_rms[:, 4 + j:5 + j], ALU.mult),
               waits=[("ve", V("ve"))], inc="ve")
            xo = 0
            OP("vector", lambda e, j=j, xo=xo: e.tensor_scalar(
                s_xnt[:, xo:xo + DM], s_xg[:], s_st[:, 16 + 2 * j:17 + 2 * j],
                s_st[:, 28 + j:29 + j], ALU.subtract, ALU.mult),
               waits=[("pe", bf.get("xnt0", 0)), ("ve", V("ve"))], inc="ve")
            xn_ve = V("ve")
            bf["xg_ve"] = xn_ve
            if ev[0] is not None:
                ev[0]()
            for ck in range(8):
                OP("tensor", lambda e, ck=ck, xo=xo: e.matmul(
                    p_ft[:, ck * 128:(ck + 1) * 128],
                    s_xnt[:, xo + ck * 128:xo + (ck + 1) * 128], s_idb[:],
                    start=True, stop=True, is_transpose=True),
                   waits=([("ve", xn_ve), ("ini", ini_n), free["ft"]]
                          if ck == 0 else ()), inc="pe")
            tr_pe = V("pe")
            bf["xnt0"] = tr_pe

            def mk_ev(j=j, tr_pe=tr_pe):
                def go():
                    OP("vector", lambda e, j=j: e.tensor_copy(
                        s_xnfm.rearrange("p (c s) -> p c s", s=TC)
                        [:, :, j * 128:(j + 1) * 128],
                        p_ft.rearrange("p (c s) -> p c s", s=128)),
                       waits=[("pe", tr_pe)], inc="ve")
                    free["ft"] = ("ve", V("ve"))
                return go
            ev[0] = mk_ev()
            yield
        ev[0]()
        W_statsc[t] = V("sc")
        OP("scalar", lambda e: e.activation(
            s_rms[:, 8:12], s_rms[:, 0:4], AF.Ln, bias=s_eps[:, 1:2], scale=1.0),
           waits=[("ve", V("ve")), ("sc", V("sc"))], inc="sc")
        OP("scalar", lambda e: e.activation(
            s_rms[:, 12:16], s_rms[:, 8:12], AF.Exp, bias=0.0, scale=-0.5),
           waits=[("sc", V("sc"))], inc="sc")
        rmsr_sc = V("sc")
        OP("tensor", lambda e: e.matmul(
            p_bc[0:4, 0:128], s_rms[:, 12:16], s_idf[:], start=True, stop=True,
            is_transpose=True), waits=[("sc", rmsr_sc), free["bc"]], inc="pe")
        OP("vector", lambda e: e.tensor_copy(s_rT[0:4, :], p_bc[0:4, 0:128]),
           waits=[("pe", V("pe"))], inc="ve")
        rT_ve = V("ve")
        for q in range(4):
            OP("tensor", lambda e, q=q: e.matmul(
                p_bc[:, q * 128:(q + 1) * 128], s_sel[0:4, q * 128:(q + 1) * 128],
                s_rT[0:4, 0:128], start=True, stop=True),
               waits=([("ve", rT_ve), ("ve", bf.get("rbc", 0))] if q == 0 else ()),
               inc="pe")
        OP("scalar", lambda e: e.activation(s_rbc[:], p_bc[:], AF.Copy, bias=0.0, scale=1.0),
           waits=[("pe", V("pe"))], inc="sc")
        rbc_sc = V("sc")
        free["bc"] = ("sc", rbc_sc)
        yield
        for ck in range(8):
            OP("vector", lambda e, ck=ck, o=xn2_o: e.scalar_tensor_tensor(
                s_xn2[:, o + ck * TC:o + (ck + 1) * TC],
                s_xnfm[:, ck * TC:(ck + 1) * TC],
                s_nw[:, ck:ck + 1], s_rbc[:], ALU.mult, ALU.mult),
               waits=[("sc", rbc_sc), ("pe", W_gatepe[t - 2] if t >= 2 else 0),
                      ("ve", V("ve"))], inc="ve")
        W_xn2[t] = V("ve")
        W_stats[t] = V("ve")
        bf["rbc"] = W_xn2[t]
        yield
        if debug and t == 0:
            for ck in range(8):
                OP("gpsimd", lambda e, ck=ck: e.dma_start(
                    out=dbg["d_xnfm"][ck], in_=s_xnfm[:, ck * TC:(ck + 1) * TC]),
                   waits=[("ve", W_xn2[t])], inc="gdd")
                OP("gpsimd", lambda e, ck=ck, o=xn2_o: e.dma_start(
                    out=dbg["d_xn2"][ck], in_=s_xn2[:, o + ck * TC:o + (ck + 1) * TC]),
                   waits=[("ve", W_xn2[t])], inc="gdd")

        def wi_load(m):
            ws = m % 2
            OP("sync", lambda e, m=m, ws=ws: e.dma_start(
                out=s_wi[:, ws * 1024:(ws + 1) * 1024], in_=wiscm[m]),
               waits=[("pe", bf.get("wis%d" % ws, 0)), ("gdw", gdw_n)],
               inc="wis%d" % ws)

        def in_proj(m):
            ws = m % 2
            wld = 16 * V("wis%d" % ws)
            pin, kin = (p_f0, "f0") if m % 2 == 0 else (p_f1, "f1")
            if m + 1 < 8:
                wi_load(m + 1)
            for k in range(8):
                OP("tensor", lambda e, m=m, k=k, ws=ws, pin=pin: e.matmul(
                    pin[:], s_wi[:, ws * 1024 + k * 128:ws * 1024 + (k + 1) * 128],
                    s_xn2[:, xn2_o + k * TC:xn2_o + (k + 1) * TC],
                    start=(k == 0), stop=(k == 7)),
                   waits=([("ve", W_xn2[t]), ("wis%d" % ws, wld), free[kin]]
                          if k == 0 else ()),
                   inc=("pe" if k == 7 else None))
            bf["wis%d" % ws] = V("pe")
            return m, V("pe"), pin, kin

        def conv_part(m, apex, pin, kin):
            slot = m % 2
            pco, kco = (p_f1, "f1") if m % 2 == 0 else (p_f0, "f0")
            OP("scalar", lambda e, slot=slot, pin=pin: e.activation(
                s_xsc[:, slot * (TC + 3) + 3:(slot + 1) * (TC + 3)], pin[:], AF.Copy,
                bias=0.0, scale=1.0),
               waits=[("pe", apex), ("pe", bf.get("xsc_pe%d" % slot, 0)),
                      ("sc", V("sc"))], inc="sc")
            ev_sc = V("sc")
            free[kin] = ("sc", ev_sc)
            OP("scalar", lambda e, slot=slot, m=m: e.activation(
                s_xsc[:, slot * (TC + 3):slot * (TC + 3) + 3],
                s_hist[:, m * 3:m * 3 + 3], AF.Copy, bias=0.0, scale=1.0),
               waits=[("gp", hist0_gp), ("sc", V("sc"))], inc="sc")
            OP("scalar", lambda e, slot=slot, m=m: e.activation(
                s_hist[:, m * 3:m * 3 + 3],
                s_xsc[:, slot * (TC + 3) + TC:slot * (TC + 3) + TC + 3],
                AF.Copy, bias=0.0, scale=1.0),
               waits=[("sc", V("sc"))], inc="sc")
            h_sc = V("sc")
            for tap in range(4):
                OP("tensor", lambda e, m=m, tap=tap, slot=slot, pco=pco: e.matmul(
                    pco[:], s_dg[:, (m * 4 + tap) * 128:(m * 4 + tap + 1) * 128],
                    s_xsc[:, slot * (TC + 3) + tap:slot * (TC + 3) + tap + TC],
                    start=(tap == 0), stop=(tap == 3)),
                   waits=([("sc", h_sc), ("gds", gds_n), free[kco]]
                          if tap == 0 else ()),
                   inc=("pe" if tap == 3 else None))
            conv_pe = V("pe")
            OP("scalar", lambda e, m=m, pco=pco, o=xact_o: e.activation(
                s_xact[:, o + m * TC:o + (m + 1) * TC], pco[:], AF.Silu,
                bias=s_cb[:, m:m + 1], scale=1.0),
               waits=[("pe", conv_pe), ("gp", sd_gp)], inc="sc")
            silu_sc = V("sc")
            free[kco] = ("sc", silu_sc)
            bf["xsc_pe%d" % slot] = conv_pe
            OP("tensor", lambda e, m=m, o=xact_o: e.matmul(
                p_db[0:96, :], s_wx[:, m * 96:(m + 1) * 96],
                s_xact[:, o + m * TC:o + (m + 1) * TC],
                start=(m == 0), stop=(m == 7)),
               waits=([("sc", silu_sc), ("gds", gds_n), free["db"]]
                      if m == 0 else [("sc", silu_sc)]),
               inc=("pe" if m == 7 else None))

        wi_load(0)
        prev = None
        for m in range(8):
            if prev is not None:
                conv_part(*prev)
            prev = in_proj(m)
            yield
        conv_part(*prev)
        W_xact[t] = V("sc")
        yield
        front_tail(t)

    def front_tail(t):
        xact_o = (t % 2) * 8 * TC
        OP("vector", lambda e: e.tensor_copy(s_xdm[0:96, :], p_db[0:96, :]),
           waits=[("pe", V("pe")), ("arin", 16 * bf.get("xdm", 0))], inc="ve")
        free["db"] = ("ve", V("ve"))
        OP("sync", lambda e, t=t: e.dma_start(out=ar_i[t], in_=s_xdm[0:96, :]),
           waits=[("ve", V("ve"))], inc="arin")
        bf["xdm"] = V("arin")
        OP("gpsimd", lambda e, t=t: e.collective_compute(
            "AllReduce", ALU.add, replica_groups=GROUPS, ins=[ar_i[t]], outs=[ar_o[t]]),
           waits=[("arin", 16 * V("arin")), ("gdw", gdw_n)], inc="cAR%d" % t)
        W_ar[t] = None
        if debug:
            for m in range(8):
                OP("gpsimd", lambda e, m=m, o=xact_o, t=t: e.dma_start(
                    out=dbg["d_xact"][t, m], in_=s_xact[:, o + m * TC:o + (m + 1) * TC]),
                   waits=[("sc", W_xact[t])], inc="gdd")

    GATESC = {}

    def gates_gen(t):
        xn2_o = (t % 2) * 8 * TC
        gate_sc = [0] * 8
        GATESC[t] = gate_sc
        def wgm_load(m):
            ws = m % 2
            OP("scalar", lambda e, m=m, ws=ws: e.dma_start(
                out=s_wgm[:, ws * 1024:(ws + 1) * 1024], in_=wigcm[m]),
               waits=[("pe", bf.get("wgm%d" % ws, 0)), ("gdw", gdw_n)],
               inc="wgm%d" % ws)

        wgm_load(0)
        psil = [None]
        for m in range(8):
            if psil[0] is not None:
                psil[0]()
                psil[0] = None
            ws = m % 2
            wld = 16 * V("wgm%d" % ws)
            if m + 1 < 8:
                wgm_load(m + 1)
            for k in range(8):
                OP("tensor", lambda e, m=m, k=k, ws=ws, o=xn2_o: e.matmul(
                    p_bc[:], s_wgm[:, ws * 1024 + k * 128:ws * 1024 + (k + 1) * 128],
                    s_xn2[:, o + k * TC:o + (k + 1) * TC],
                    start=(k == 0), stop=(k == 7)),
                   waits=([("wgm%d" % ws, wld), ("ve", W_xn2[t]), free["bc"]]
                          if k == 0 else ()),
                   inc=("pe" if k == 7 else None))
            bf["wgm%d" % ws] = V("pe")
            g_pe = V("pe")

            def mk_sil(m=m, g_pe=g_pe):
                def go():
                    OP("scalar", lambda e, m=m: e.activation(
                        s_xf[:, m * TC:(m + 1) * TC], p_bc[:], AF.Silu,
                        bias=0.0, scale=1.0),
                       waits=[("pe", g_pe), ("rsi", 16 * bf.get("xf_sy", 0)),
                              ("sc", V("sc"))], inc="sc")
                    gate_sc[m] = V("sc")
                    free["bc"] = ("sc", V("sc"))
                return go
            psil[0] = mk_sil()
            yield
        psil[0]()
        W_gatepe[t] = V("pe")

    def outproj_gen(t):
        if debug:
            for m in range(8):
                OP("gpsimd", lambda e, t=t, m=m: e.dma_start(
                    out=dbg["d_xfop"][t, m], in_=s_xf[:, m * TC:(m + 1) * TC]),
                   waits=[("ve", W_xfve[t])], inc="gdd")
        for dm in range(8):
            ws = dm % 2
            OP("sync", lambda e, dm=dm, ws=ws: e.dma_start(
                out=s_wom[:, ws * 1024:(ws + 1) * 1024],
                in_=woutcm[dm]),
               waits=[("pe", bf.get("wom%d" % ws, 0)), ("gdw", gdw_n)],
               inc="wom%d" % ws)
            wld = 16 * V("wom%d" % ws)
            for k in range(8):
                OP("tensor", lambda e, k=k, ws=ws: e.matmul(
                    p_do[:], s_wom[:, ws * 1024 + k * 128:ws * 1024 + (k + 1) * 128],
                    s_xf[:, k * TC:(k + 1) * TC], start=(k == 0), stop=(k == 7)),
                   waits=([("wom%d" % ws, wld), ("ve", W_xfve[t]), free["do"]]
                          if k == 0 else ()),
                   inc=("pe" if k == 7 else None))
            bf["wom%d" % ws] = V("pe")
            OP("scalar", lambda e: e.activation(s_opb[:], p_do[:], AF.Copy, bias=0.0,
                                                scale=1.0),
               waits=[("pe", V("pe")), ("rsi", 16 * bf.get("opb", 0)),
                      ("sc", V("sc"))], inc="sc")
            free["do"] = ("sc", V("sc"))
            OP("scalar", lambda e, t=t, dm=dm: e.dma_start(
                out=rs_i[t // 2][t % 2][dm * 128:(dm + 1) * 128, :], in_=s_opb[:]),
               waits=[("sc", V("sc"))], inc="rsi")
            bf["opb"] = V("rsi")
            yield
        bf["xf_sy"] = V("rsi")
        if t % 2 == 1:
            OP("gpsimd", lambda e, i=t // 2: e.collective_compute(
                "ReduceScatter", ALU.add, replica_groups=GROUPS,
                ins=[rs_i[i]], outs=[rs_o[i]]),
               waits=[("rsi", 16 * V("rsi"))], inc="cRS%d" % (t // 2))
            W_rs[t // 2] = None
            rs_issued.add(t // 2)

    DELDONE = set()
    W_delsc = {}

    def delta_gen(t):
        OP("sync", lambda e, t=t: e.dma_start(
            out=s_xdf[0:64, :], in_=ar_o[t][0:64, :]),
           waits=[("cAR%d" % t, 1), ("pe", bf.get("xdf0", 0))], inc="xdf")
        ld = 16 * V("xdf")
        if debug:
            OP("gpsimd", lambda e, t=t: e.dma_start(
                out=dbg["d_xdbl"][t], in_=ar_o[t]),
               waits=[("xdf", ld)], inc="gdd")
        for m in range(8):
            OP("tensor", lambda e, m=m: e.matmul(
                p_do[:], s_wdt[0:64, m * 128:(m + 1) * 128], s_xdf[0:64, :],
                start=True, stop=True),
               waits=[("xdf", ld), ("gds", gds_n), free["do"]], inc="pe")
            OP("vector", lambda e, m=m: e.tensor_scalar(
                s_sfs[:], p_do[:], s_bdt[:, m:m + 1], SP_CLAMP, ALU.add, ALU.min),
               waits=[("pe", V("pe")), ("sc", bf.get("sfs", 0))], inc="ve")
            free["do"] = ("ve", V("ve"))
            OP("scalar", lambda e, m=m: e.activation(
                s_sfs[:], s_sfs[:], AF.Exp, bias=s_bdt[:, m:m + 1], scale=1.0),
               waits=[("ve", V("ve"))], inc="sc")
            OP("scalar", lambda e, m=m, do=(t % 2) * 8 * TC: e.activation(
                s_del[:, do + m * TC:do + (m + 1) * TC], s_sfs[:], AF.Ln,
                bias=1.0, scale=1.0),
               waits=[("gp", W_dugp.get(t - 2, 0)), ("sc", V("sc"))], inc="sc")
            bf["sfs"] = V("sc")
            if debug:
                OP("gpsimd", lambda e, t=t, m=m, do=(t % 2) * 8 * TC: e.dma_start(
                    out=dbg["d_delta"][t, m],
                    in_=s_del[:, do + m * TC:do + (m + 1) * TC]),
                   waits=[("sc", V("sc"))], inc="gdd")
            if m % 2 == 1:
                yield
        W_delsc[t] = V("sc")
        bf["xdf0"] = V("pe")
        DELDONE.add(t)

    def scan(t):
        xact_o = (t % 2) * 8 * TC

        if t not in DELDONE:
            for _ in delta_gen(t):
                pass
        for m in range(8):
            OP("gpsimd", lambda e, m=m, o=xact_o, do=(t % 2) * 8 * TC:
               e.tensor_tensor(
                   s_du[:, m * TC:(m + 1) * TC],
                   s_del[:, do + m * TC:do + (m + 1) * TC],
                   s_xact[:, o + m * TC:o + (m + 1) * TC], ALU.mult),
               waits=[("sc", W_delsc[t]), ("sc", W_xact[t]),
                      ("ve", W_nloopve.get(t - 1, 0))], inc="gp")
            OP("scalar", lambda e, m=m, o=xact_o: e.activation(
                s_y[:, m * TC:(m + 1) * TC], s_xact[:, o + m * TC:o + (m + 1) * TC],
                AF.Copy, bias=0.0, scale=s_D[:, m:m + 1]),
               waits=[("ve", W_xfve.get(t - 1, 0))], inc="sc")
        du_gp = V("gp")
        yinit_sc = V("sc")
        W_dugp[t] = du_gp
        W_xactfree[t] = du_gp

        if t >= 1:
            hi.append(outproj_gen(t - 1))
        hi.append(gates_gen(t))
        if t + 1 < NT:
            hi.append(delta_gen(t + 1))
        if t + 2 < NT:
            hi.append(front_gen(t + 2))

        def bc_load(n):
            bs = n % 4
            OP("sync", lambda e, t=t, n=n, bs=bs: e.dma_start(
                out=s_Bb[:, bs * TC:(bs + 1) * TC],
                in_=ar_o[t][64 + n:65 + n, :].partition_broadcast(128)),
               waits=[("cAR%d" % t, 1), ("ve", bf.get("Bb%d" % bs, 0))],
               inc="bB%d" % bs)
            OP("sync", lambda e, t=t, n=n, bs=bs: e.dma_start(
                out=s_Cb[:, bs * TC:(bs + 1) * TC],
                in_=ar_o[t][80 + n:81 + n, :].partition_broadcast(128)),
               waits=[("ve", bf.get("Cb%d" % bs, 0)),
                      ("gp", bf.get("Cbgp%d" % bs, 0))], inc="bC%d" % bs)

        for n in range(3):
            bc_load(n)
        for n in range(NS):
            if n + 3 < NS:
                bc_load(n + 3)
            bs = n % 4
            bcB_n = 16 * V("bB%d" % bs)
            bcC_n = 16 * V("bC%d" % bs)
            for m in range(8):
                a = (n * 8 + m) % 3
                OP("scalar", lambda e, m=m, n=n, a=a, do=(t % 2) * 8 * TC:
                   e.activation(
                       s_dA[:, a * TC:(a + 1) * TC],
                       s_del[:, do + m * TC:do + (m + 1) * TC], AF.Exp,
                       bias=0.0, scale=s_An[:, m * 16 + n:m * 16 + n + 1]),
                   waits=[("ve", max(bf.get("dA%d" % a, 0), aneg_ve))], inc="sc")
                dA_sc = V("sc")
                OP("vector", lambda e, m=m, a=a, bs=bs: e.tensor_tensor(
                    s_X[:, a * TC:(a + 1) * TC], s_du[:, m * TC:(m + 1) * TC],
                    s_Bb[:, bs * TC:(bs + 1) * TC], ALU.mult),
                   waits=[("gp", du_gp), ("gp", bf.get("pr_gp%d" % a, 0)),
                          ("bB%d" % bs, bcB_n)], inc="ve")
                X_ve = V("ve")
                init = 0.0 if t == 0 else s_car[:, m * 16 + n:m * 16 + n + 1]
                OP("vector", lambda e, m=m, a=a, init=init: e.tensor_tensor_scan(
                    s_h[:, m * TC:(m + 1) * TC], s_dA[:, a * TC:(a + 1) * TC],
                    s_X[:, a * TC:(a + 1) * TC], init, ALU.mult, ALU.add),
                   waits=[("sc", dA_sc), ("gp", max(bf.get("hgp%d" % m, 0),
                                                     W_carn.get((t - 1, n), 0),
                                                     W_carn.get((t, n - 1), 0))),
                          ("ve", X_ve)], inc="ve")
                scan_ve = V("ve")
                bf["dA%d" % a] = scan_ve
                if debug and n == 0:
                    OP("gpsimd", lambda e, t=t, m=m: e.dma_start(
                        out=dbg["d_h0"][t, m], in_=s_h[:, m * TC:(m + 1) * TC]),
                       waits=[("ve", scan_ve)], inc="gdd")
                OP("gpsimd", lambda e, m=m, a=a, bs=bs: e.tensor_tensor(
                    s_pr[:, a * TC:(a + 1) * TC], s_Cb[:, bs * TC:(bs + 1) * TC],
                    s_h[:, m * TC:(m + 1) * TC], ALU.mult),
                   waits=[("gp", bf.get("pr_gp%d" % a, 0)), ("ve", V("ve")),
                          ("bC%d" % bs, bcC_n)], inc="gp")
                pr_ve = V("ve")
                OP("gpsimd", lambda e, m=m, a=a: e.tensor_tensor(
                    s_y[:, m * TC:(m + 1) * TC], s_y[:, m * TC:(m + 1) * TC],
                    s_pr[:, a * TC:(a + 1) * TC], ALU.add),
                   waits=[("ve", pr_ve), ("sc", yinit_sc), ("gp", V("gp"))],
                   inc="gp")
                bf["pr_gp%d" % a] = V("gp")
                bf["hgp%d" % m] = V("gp")
            OP("gpsimd", lambda e, n=n: e.tensor_copy(
                s_car.rearrange("p (m x) -> p m x", x=16)[:, :, n:n + 1],
                s_h.rearrange("p (m c) -> p m c", c=TC)[:, :, TC - 1:TC]),
               waits=[("gp", V("gp")), ("ve", V("ve"))], inc="gp")
            W_carn[(t, n)] = V("gp")
            bf["Bb%d" % bs] = V("ve")
            bf["Cb%d" % bs] = V("ve")
            bf["Cbgp%d" % bs] = V("gp")
            if n >= 5:
                PUMP[0](1, lo_max=0)
            if n >= 10:
                PUMP[0](1, lo_max=0)
            if n >= 12:
                PUMP[0](1, lo_max=1, hi_max=0)
            PUMP[0](4)
        ya_gp = V("gp")
        W_nloopve[t] = V("ve")
        while hi:
            try:
                next(hi[0])
            except StopIteration:
                hi.pop(0)
        if debug:
            for m in range(8):
                OP("gpsimd", lambda e, t=t, m=m: e.dma_start(
                    out=dbg["d_y"][t, m], in_=s_y[:, m * TC:(m + 1) * TC]),
                   waits=[("gp", ya_gp)], inc="gdd")

        for m in range(8):
            OP("vector", lambda e, m=m: e.tensor_tensor(
                s_xf[:, m * TC:(m + 1) * TC], s_y[:, m * TC:(m + 1) * TC],
                s_xf[:, m * TC:(m + 1) * TC], ALU.mult),
               waits=[("sc", GATESC[t][m]), ("gp", ya_gp), ("ve", V("ve"))], inc="ve")
        xf_ve = V("ve")
        W_xfve[t] = xf_ve
        if debug:
            for m in range(8):
                OP("gpsimd", lambda e, t=t, m=m: e.dma_start(
                    out=dbg["d_xf"][t, m], in_=s_xf[:, m * TC:(m + 1) * TC]),
                   waits=[("ve", xf_ve)], inc="gdd")

    def fin_gen(i):
        eloc = 0 if i < 2 else 1
        fev = [None]
        for j in range(4):
            r0 = i * TC + j * 128
            xo = (j % 2) * DM
            OP("scalar", lambda e, r0=r0, xo=xo: e.dma_start(
                out=s_xfi[:, xo:xo + DM], in_=x_fin[r0:r0 + 128, :]),
               waits=[("ve", bf.get("xfi%d" % (j % 2), 0)),
                      ("sc", bf.get("xfi_sc%d" % (j % 2), 0))], inc="xfi%d" % (j % 2))
            ld = 16 * V("xfi%d" % (j % 2))
            OP("vector", lambda e, xo=xo: e.bn_stats(
                s_fst[:, 0:6], s_xfi[:, xo:xo + 512]),
               waits=[("xfi%d" % (j % 2), ld),
                      ("sc", W_fstatsc.get(i - 1, 0)), ("ve", V("ve"))], inc="ve")
            OP("vector", lambda e, xo=xo: e.bn_stats(
                s_fst[:, 6:12], s_xfi[:, xo + 512:xo + 1024]),
               waits=[("ve", V("ve"))], inc="ve")
            OP("vector", lambda e, j=j: e.bn_aggr(
                s_fst[:, 16 + 2 * j:18 + 2 * j], s_fst[:, 0:12]),
               waits=[("ve", V("ve"))], inc="ve")
            OP("scalar", lambda e, j=j: e.activation(
                s_fst[:, 24 + j:25 + j], s_fst[:, 17 + 2 * j:18 + 2 * j],
                AF.Ln, bias=s_eps[:, 0:1], scale=1.0),
               waits=[("ve", V("ve")), ("sc", V("sc"))], inc="sc")
            OP("scalar", lambda e, j=j: e.activation(
                s_fst[:, 17 + 2 * j:18 + 2 * j], s_fst[:, 24 + j:25 + j],
                AF.Exp, bias=0.0, scale=-0.5),
               waits=[("sc", V("sc"))], inc="sc")
            OP("vector", lambda e, j=j, xo=xo: e.tensor_scalar(
                s_xnf[:], s_xfi[:, xo:xo + DM], s_fst[:, 16 + 2 * j:17 + 2 * j],
                s_fst[:, 17 + 2 * j:18 + 2 * j], ALU.subtract, ALU.mult),
               waits=[("pe", bf.get("xnf", 0)), ("sc", V("sc")), ("ve", V("ve"))],
               inc="ve")
            xnf_ve = V("ve")
            bf["xfi%d" % (j % 2)] = xnf_ve
            if fev[0] is not None:
                fev[0]()
            trs = []
            for ckh in range(2):
                pb, kb = (p_n0, "n0") if ckh == 0 else (p_n1, "n1")
                for q in range(4):
                    ck = ckh * 4 + q
                    OP("tensor", lambda e, ck=ck, q=q, pb=pb: e.matmul(
                        pb[:, q * 128:(q + 1) * 128], s_xnf[:, ck * 128:(ck + 1) * 128],
                        s_idf[:], start=True, stop=True, is_transpose=True),
                       waits=([("ve", xnf_ve), free[kb]] if q == 0 else ()), inc="pe")
                trs.append((ckh, pb, kb, V("pe")))
            bf["xnf"] = V("pe")

            def mk_fev(j=j, trs=trs):
                def go():
                    for ckh, pb, kb, tr_pe in trs:
                        OP("vector", lambda e, j=j, ckh=ckh, pb=pb: e.tensor_copy(
                            s_xnff.rearrange("p (c s) -> p c s", s=TC)
                            [:, ckh * 4:(ckh + 1) * 4, j * 128:(j + 1) * 128],
                            pb.rearrange("p (c s) -> p c s", s=128)[:, 0:4, :]),
                           waits=[("pe", tr_pe), ("pe", bf.get("xnff", 0))], inc="ve")
                        free[kb] = ("ve", V("ve"))
                return go
            fev[0] = mk_fev()
            yield
        fev[0]()
        W_fstatsc[i] = V("sc")
        xnff_ve = V("ve")
        W_fstats[i] = V("ve")
        yield

        for sub in range(2):
            gel = [None]
            for m in range(32):
                if gel[0] is not None:
                    gel[0]()
                ws = m % 4
                OP("sync" if ws % 2 == 0 else "scalar",
                   lambda e, m=m, ws=ws, eloc=eloc: e.dma_start(
                       out=s_w1m[:, ws * 1024:(ws + 1) * 1024],
                       in_=w1cm[eloc][m]),
                   waits=[("pe", bf.get("w1m%d" % ws, 0)),
                          ("gwf%d" % eloc, gwf_n[eloc])],
                   inc="w1m%d" % ws)
                wld = 16 * V("w1m%d" % ws)
                pb, kb = (p_n0, "n0") if m % 2 == 0 else (p_n1, "n1")
                for k in range(8):
                    OP("tensor", lambda e, m=m, k=k, ws=ws, sub=sub, pb=pb: e.matmul(
                        pb[:, 0:256],
                        s_w1m[:, ws * 1024 + k * 128:ws * 1024 + (k + 1) * 128],
                        s_xnff.rearrange("p (c s) -> p c s", s=TC)
                        [:, k, sub * 256:(sub + 1) * 256],
                        start=(k == 0), stop=(k == 7)),
                       waits=([("w1m%d" % ws, wld), ("ve", xnff_ve), free[kb]]
                              if k == 0 else ()),
                       inc=("pe" if k == 7 else None))
                bf["w1m%d" % ws] = V("pe")
                w1_pe = V("pe")

                def mk_gel(m=m, pb=pb, kb=kb, w1_pe=w1_pe):
                    def go():
                        OP("scalar", lambda e, m=m, eloc=eloc, pb=pb: e.activation(
                            s_hff[:, m * 256:(m + 1) * 256], pb[:, 0:256],
                            AF.Gelu_apprx_tanh,
                            bias=s_b1[:, eloc * 32 + m:eloc * 32 + m + 1], scale=1.0),
                           waits=[("pe", w1_pe),
                                  ("pe", bf.get("hff", 0) if m == 0 else 0)],
                           inc="sc")
                        free[kb] = ("sc", V("sc"))
                    return go
                gel[0] = mk_gel()
                if m % 2 == 1:
                    yield
            gel[0]()
            hff_sc = V("sc")
            if debug and i == 0 and sub == 0:
                for m in range(8):
                    OP("gpsimd", lambda e, m=m: e.dma_start(
                        out=dbg["d_hff"][m], in_=s_hff[:, m * 256:(m + 1) * 256]),
                       waits=[("sc", hff_sc)], inc="gdd")
            if sub == 0:
                while i not in rs_issued:
                    yield "blk"
                if debug:
                    OP("gpsimd", lambda e, i=i: e.dma_start(
                        out=dbg["d_rso"][i], in_=rs_o[i]),
                       waits=[("cRS%d" % i, 1)], inc="gdd")
            for dmh in range(2):
                pomf = []
                for q in range(4):
                    if len(pomf) >= 2:
                        pomf.pop(0)()
                    dm = dmh * 4 + q
                    ws = dm % 2
                    for hf in range(2):
                        OP("sync" if hf == 0 else "scalar",
                           lambda e, dm=dm, hf=hf, eloc=eloc: e.dma_start(
                               out=s_w2d[:, hf * 2048:(hf + 1) * 2048],
                               in_=w2cm[eloc][dm][:, hf * 16:(hf + 1) * 16, :]),
                           waits=[("pe", bf.get("w2d%d" % hf, 0)),
                                  ("gwf%d" % eloc, gwf_n[eloc])],
                           inc="w2d%d" % hf)
                    wld = (16 * V("w2d0"), 16 * V("w2d1"))
                    pb, kb = (p_n0, "n0") if dm % 2 == 0 else (p_n1, "n1")
                    OP("scalar", lambda e, i=i, dm=dm: e.dma_start(
                        out=s_rs[:, (dm % 2) * TC:(dm % 2 + 1) * TC],
                        in_=rs_o[i][dm * 128:(dm + 1) * 128, :]),
                       waits=[("cRS%d" % i, 1), ("pe", bf.get("rs%d" % (dm % 2), 0))],
                       inc="rso%d" % (dm % 2))
                    rld = 16 * V("rso%d" % (dm % 2))
                    for k in range(32):
                        hf = k // 16
                        OP("tensor", lambda e, k=k, hf=hf, sub=sub, pb=pb: e.matmul(
                            pb[:, 0:256],
                            s_w2d[:, hf * 2048 + (k % 16) * 128:hf * 2048 + (k % 16 + 1) * 128],
                            s_hff[:, k * 256:(k + 1) * 256],
                            start=(k == 0), stop=False),
                           waits=([("w2d0", wld[0]), ("sc", hff_sc), free[kb]]
                                  if k == 0 else
                                  ([("w2d1", wld[1])] if k == 16 else ())),
                           inc=None)
                    OP("tensor", lambda e, dm=dm, sub=sub, pb=pb: e.matmul(
                        pb[:, 0:256], s_idb[:],
                        s_rs[:, (dm % 2) * TC + sub * 256:(dm % 2) * TC + (sub + 1) * 256],
                        start=False, stop=False),
                       waits=[("rso%d" % (dm % 2), rld)], inc=None)
                    OP("tensor", lambda e, dm=dm, sub=sub, pb=pb: e.matmul(
                        pb[:, 0:256], s_idb[:],
                        s_xnff.rearrange("p (c s) -> p c s", s=TC)
                        [:, dm, sub * 256:(sub + 1) * 256],
                        start=False, stop=True), inc="pe")
                    bf["w2d0"] = V("pe"); bf["w2d1"] = V("pe")
                    bf["rs%d" % (dm % 2)] = V("pe")
                    w2_pe = V("pe")

                    def mk_omf(q=q, dm=dm, pb=pb, kb=kb, w2_pe=w2_pe):
                        def go():
                            OP("vector", lambda e, dm=dm, q=q, eloc=eloc, pb=pb:
                               e.tensor_scalar(
                                   s_omf[:, q * 256:(q + 1) * 256], pb[:, 0:256],
                                   s_b2[:, eloc * 8 + dm:eloc * 8 + dm + 1], None,
                                   ALU.add),
                               waits=[("pe", w2_pe),
                                      ("pe", bf.get("omf", 0) if q == 0 else 0)],
                               inc="ve")
                            free[kb] = ("ve", V("ve"))
                        return go
                    pomf.append(mk_omf())
                    yield
                for f in pomf:
                    f()
                pomf = []
                omf_ve = V("ve")

                def mk_xadd(jj):
                    j = sub * 2 + jj
                    xo = (j % 2) * DM
                    pb, kb = (p_n0, "n0") if jj == 0 else (p_n1, "n1")
                    OP("scalar", lambda e, j=j, xo=xo, i=i: e.dma_start(
                        out=s_xfi[:, xo:xo + DM],
                        in_=x_fin[i * TC + j * 128:i * TC + (j + 1) * 128, :]),
                       waits=[("ve", bf.get("xfi%d" % (j % 2), 0)),
                              ("sc", bf.get("xfi_sc%d" % (j % 2), 0))],
                       inc="xfi%d" % (j % 2))
                    xld = 16 * V("xfi%d" % (j % 2))
                    for q in range(4):
                        OP("tensor", lambda e, q=q, jj=jj, pb=pb: e.matmul(
                            pb[:, q * 128:(q + 1) * 128],
                            s_omf[:, q * 256 + jj * 128:q * 256 + (jj + 1) * 128],
                            s_idf[:], start=True, stop=True, is_transpose=True),
                           waits=([("ve", omf_ve), free[kb]] if q == 0 else ()),
                           inc="pe")
                    tr_pe = V("pe")
                    bf["omf"] = tr_pe

                    def go(j=j, xo=xo, pb=pb, kb=kb, tr_pe=tr_pe, xld=xld,
                           dmh=dmh, i=i):
                        OP("vector", lambda e, xo=xo, dmh=dmh, pb=pb: e.tensor_tensor(
                            s_otk[:, 0:TC],
                            s_xfi[:, xo + dmh * TC:xo + (dmh + 1) * TC], pb[:],
                            ALU.add),
                           waits=[("pe", tr_pe), ("xfi%d" % (j % 2), xld),
                                  ("out", 16 * bf.get("otk0", 0))], inc="ve")
                        free[kb] = ("ve", V("ve"))
                        bf["xfi%d" % (j % 2)] = V("ve")
                        OP("scalar", lambda e, i=i, j=j, dmh=dmh: e.dma_start(
                            out=out[i * TC + j * 128:i * TC + (j + 1) * 128,
                                    dmh * TC:(dmh + 1) * TC],
                            in_=s_otk[:, 0:TC]),
                           waits=[("ve", V("ve"))], inc="out")
                        bf["otk0"] = V("out")
                    return go
                xa = mk_xadd(0)
                yield
                xa()
                xa = mk_xadd(1)
                yield
                xa()
            bf["hff"] = V("pe")
        bf["xnff"] = V("pe")

    def pump(k=2, lo_max=2, hi_max=2):
        lo_n = 0
        hi_n = 0
        last_gen = [None]
        for _ in range(k):
            while hi and hi_n < hi_max and hi[0] is not last_gen[0]:
                try:
                    g = hi[0]
                    next(g)
                    last_gen[0] = g
                    hi_n += 1
                    break
                except StopIteration:
                    hi.pop(0)
            else:
                if lo_n >= lo_max:
                    return
                while lo:
                    try:
                        r = next(lo[0])
                    except StopIteration:
                        lo.pop(0)
                        continue
                    if isinstance(r, str):
                        return
                    lo_n += 1
                    break
                else:
                    return

    def front_full(t):
        for _ in front_gen(t):
            pass

    PUMP[0] = pump
    front_full(0)
    front_full(1)
    lo.append(fin_gen(0))
    scan(0)
    scan(1)
    lo.append(fin_gen(1))
    scan(2)
    scan(3)
    lo.append(fin_gen(2))
    scan(4)
    scan(5)
    lo.append(fin_gen(3))
    scan(6)
    scan(7)
    hi.append(outproj_gen(7))
    while hi:
        try:
            next(hi[0])
        except StopIteration:
            hi.pop(0)
    while lo:
        try:
            r = next(lo[0])
        except StopIteration:
            lo.pop(0)
            continue
        if isinstance(r, str):
            raise RuntimeError("fin blocked after all RS issued")

    end_waits = [("out", 16 * V("out"))]
    if debug:
        end_waits.append(("gdd", 16 * V("gdd")))
    OP("gpsimd", lambda e: e.nop(), waits=end_waits)

    em.finalize(blk)
    es.close()
    return nc, dbg



def _prep_inputs(inputs):
    x = np.asarray(inputs["L0_attn_x"])
    norm_w = np.asarray(inputs["L0_attn_norm_w"])
    w_in = np.asarray(inputs["L0_attn_w_in"])
    conv_w = np.asarray(inputs["L0_attn_conv_w"])
    conv_b = np.asarray(inputs["L0_attn_conv_b"])
    w_x = np.asarray(inputs["L0_attn_w_x"])
    w_dt = np.asarray(inputs["L0_attn_w_dt"])
    b_dt = np.asarray(inputs["L0_attn_b_dt"])
    A_log = np.asarray(inputs["L0_attn_A_log"])
    D = np.asarray(inputs["L0_attn_D"])
    w_out = np.asarray(inputs["L0_attn_w_out"])
    w1 = np.asarray(inputs["L0_ffn_w1"])
    b1 = np.asarray(inputs["L0_ffn_b1"])
    w2 = np.asarray(inputs["L0_ffn_w2"])
    b2 = np.asarray(inputs["L0_ffn_b2"])

    identf = np.eye(128, dtype=np.float32)
    identb = np.eye(128).astype(ml_dtypes.bfloat16)
    sel16 = np.kron(np.eye(16), np.ones((1, 128))).astype(ml_dtypes.bfloat16)

    def cols(v):
        return np.ascontiguousarray(v.reshape(8, 128).T.astype(np.float32))

    in_maps = []
    for c in range(8):
        b, g = c // 2, c % 2
        lo = g * DH
        xg = np.ascontiguousarray(x[b].astype(np.float32))
        slabs = [xg[(2 * i + g) * TC:(2 * i + g + 1) * TC] for i in range(4)]
        x_fin_ = np.ascontiguousarray(np.concatenate(slabs, axis=0))
        dg = np.zeros((32, 128, 128), np.float32)
        cw = conv_w[lo:lo + DH, 0, :]
        for m in range(8):
            for tap in range(4):
                np.fill_diagonal(dg[m * 4 + tap], cw[m * 128:(m + 1) * 128, tap])
        Al = A_log[lo:lo + DH].reshape(8, 128, NS).transpose(1, 0, 2).reshape(128, 128)
        e0, e1 = 2 * b, 2 * b + 1
        in_maps.append({
            "x_g": xg,
            "x_fin": x_fin_,
            "wi_ssm": np.ascontiguousarray(w_in[:, lo:lo + DH]),
            "wi_gate": np.ascontiguousarray(w_in[:, 2048 + lo:2048 + lo + DH]),
            "w_x_h": np.ascontiguousarray(w_x[lo:lo + DH, :]),
            "w_dt_h": np.ascontiguousarray(w_dt[:, lo:lo + DH]),
            "w_out_h": np.ascontiguousarray(w_out[lo:lo + DH, :]),
            "diag": dg,
            "cb_col": cols(conv_b[lo:lo + DH]),
            "bdt_col": cols(b_dt[lo:lo + DH]),
            "D_col": cols(D[lo:lo + DH]),
            "nw_col": cols(norm_w),
            "A_lay": np.ascontiguousarray(Al.astype(np.float32)),
            "w1_a": np.ascontiguousarray(w1[e0].astype(ml_dtypes.bfloat16)),
            "w1_b": np.ascontiguousarray(w1[e1].astype(ml_dtypes.bfloat16)),
            "w2_a": np.ascontiguousarray(w2[e0].astype(ml_dtypes.bfloat16)),
            "w2_b": np.ascontiguousarray(w2[e1].astype(ml_dtypes.bfloat16)),
            "b1_cols": np.ascontiguousarray(np.concatenate(
                [b1[e0].reshape(32, 128).T, b1[e1].reshape(32, 128).T], axis=1
            ).astype(np.float32)),
            "b2_cols": np.ascontiguousarray(np.concatenate(
                [b2[e0].reshape(8, 128).T, b2[e1].reshape(8, 128).T], axis=1
            ).astype(np.float32)),
            "identf": identf, "identb": identb,
            "sel16": np.ascontiguousarray(sel16),
        })
    return in_maps


_NC_CACHE = {}


def kernel(**inputs):
    in_maps = _prep_inputs(inputs)
    if "nc" not in _NC_CACHE:
        _NC_CACHE["nc"], _ = build_nc(debug=False)
    nc = _NC_CACHE["nc"]
    res = run_bass_kernel_spmd(nc, in_maps, core_ids=list(range(8)))
    o = np.empty((4, L, DM), np.float32)
    for c in range(8):
        b, g = c // 2, c % 2
        r = res.results[c]["o"]
        for i in range(4):
            o[b, (2 * i + g) * TC:(2 * i + g + 1) * TC] = r[i * TC:(i + 1) * TC]
    return o


# version 37
# speedup vs baseline: 1.0197x; 1.0197x over previous
import contextlib
import numpy as np
import ml_dtypes

import concourse.bass as bass
import concourse.mybir as mybir
from concourse.bass_utils import run_bass_kernel_spmd

F32 = mybir.dt.float32
BF16 = mybir.dt.bfloat16
AF = mybir.ActivationFunctionType
ALU = mybir.AluOpType
AX = mybir.AxisListType.X

DM = 1024
DH = 1024
NS = 16
L = 4096
TC = 512
NT = 8
DFF = 4096
EPS_LN = 1e-5
EPS_RMS = float(np.finfo(np.float32).eps)
SP_CLAMP = 30.0
GROUPS = [[0, 1], [2, 3], [4, 5], [6, 7]]

ENGS = ("sync", "tensor", "vector", "scalar", "gpsimd")


class Em:
    def __init__(self, nc):
        self.nc = nc
        self.streams = {e: [] for e in ENGS}
        self.sems = {}
        self.cnt = {}
        self.eng_of = {}
        self.last_wait = {e: {} for e in ENGS}
        self.drained = {}
        self.es = contextlib.ExitStack()

    def chan(self, name, engine):
        if name not in self.sems:
            self.sems[name] = self.es.enter_context(self.nc.semaphore(name))
            self.cnt[name] = 0
            self.eng_of[name] = engine
        return self.sems[name]

    def op(self, engine, build, waits=(), inc=None, inc_by=1):
        if inc is not None:
            self.chan(inc, engine)
            self.cnt[inc] += inc_by
        lw = self.last_wait[engine]
        need_drain = False
        real = []
        for w in waits:
            if w is None:
                continue
            nm, val = w
            if val <= 0:
                continue
            if self.eng_of.get(nm) == engine and nm == inc:
                val = min(val, self.cnt[nm] - inc_by)
                if val <= 0:
                    continue
            if lw.get(nm, -1) >= val:
                continue
            lw[nm] = val
            real.append((nm, val))
        sems = self.sems

        step = 1 if (inc in ("pe", "ve", "sc", "gp") or
                     (inc is not None and inc.startswith("c"))) else 16

        def fn(eng, real=tuple(real), inc=inc, inc_by=inc_by, build=build,
               step=step, need_drain=need_drain):
            if need_drain:
                eng.drain()
            for nm, val in real:
                eng.wait_ge(sems[nm], val)
            ins = build(eng)
            if inc is not None and ins is not None:
                ins.then_inc(sems[inc], step * inc_by)
        self.streams[engine].append(fn)

    def v(self, name):
        return self.cnt.get(name, 0)

    def finalize(self, block):
        for engine in ENGS:
            stream = self.streams[engine]

            def run(eng, stream=stream):
                for fn in stream:
                    fn(eng)
            getattr(block, engine)(run)


def build_nc(debug=False):
    nc = bass.Bass("TRN2", target_bir_lowering=False)
    em = Em(nc)
    es = em.es

    def din(name, shape, dtype=F32):
        return nc.dram_tensor(name, list(shape), dtype, kind="ExternalInput").ap()

    x_g = din("x_g", (L, DM))
    x_fin = din("x_fin", (4 * TC, DM))
    wi_ssm = din("wi_ssm", (DM, DH))
    wi_gate = din("wi_gate", (DM, DH))
    w_x_h = din("w_x_h", (DH, 96))
    w_dt_h = din("w_dt_h", (64, DH))
    w_out_h = din("w_out_h", (DH, DM))
    diag = din("diag", (32, 128, 128))
    cb_col = din("cb_col", (128, 8))
    bdt_col = din("bdt_col", (128, 8))
    D_col = din("D_col", (128, 8))
    nw_col = din("nw_col", (128, 8))
    A_lay = din("A_lay", (128, 128))
    w1_a = din("w1_a", (DM, DFF), BF16); w1_b = din("w1_b", (DM, DFF), BF16)
    w2_a = din("w2_a", (DFF, DM), BF16); w2_b = din("w2_b", (DFF, DM), BF16)
    b1_cols = din("b1_cols", (128, 64))
    b2_cols = din("b2_cols", (128, 16))
    identf = din("identf", (128, 128))
    identb = din("identb", (128, 128), BF16)
    sel16 = din("sel16", (16, 16 * 128), BF16)

    out = nc.dram_tensor("o", [4 * TC, DM], F32, kind="ExternalOutput").ap()

    dbg = {}
    if debug:
        def dout(name, shape, dtype=F32):
            dbg[name] = nc.dram_tensor(name, list(shape), dtype, kind="ExternalOutput").ap()
        dout("d_xnfm", (8, 128, TC), BF16)
        dout("d_xn2", (8, 128, TC), BF16)
        dout("d_xact", (NT, 8, 128, TC), BF16)
        dout("d_xdbl", (NT, 96, TC), BF16)
        dout("d_delta", (NT, 8, 128, TC), BF16)
        dout("d_y", (NT, 8, 128, TC), BF16)
        dout("d_xf", (NT, 8, 128, TC), BF16)
        dout("d_h0", (NT, 8, 128, TC), BF16)
        dout("d_hff", (8, 128, 256), BF16)
        dout("d_xfop", (NT, 8, 128, TC), BF16)
        dout("d_rso", (4, DM, TC), BF16)

    def dint(name, shape, dtype):
        return nc.dram_tensor(name, list(shape), dtype, kind="Internal").ap()

    wigcm = dint("wigcm", (8, 128, 8, 128), BF16)
    wiscm = dint("wiscm", (8, 128, 8, 128), BF16)
    woutcm = dint("woutcm", (8, 128, 8, 128), BF16)
    w1cm = [dint(f"w1cm{e}", (32, 128, 8, 128), BF16) for e in range(2)]
    w2cm = [dint(f"w2cm{e}", (8, 128, 32, 128), BF16) for e in range(2)]
    ar_i = [dint(f"ar_i{t}", (96, TC), BF16) for t in range(NT)]
    ar_o = [dint(f"ar_o{t}", (96, TC), BF16) for t in range(NT)]
    rs_i = [dint(f"rs_i{i}", (2, DM, TC), BF16) for i in range(4)]
    rs_o = [dint(f"rs_o{i}", (DM, TC), BF16) for i in range(4)]

    def sb(name, cols, dtype=F32):
        return es.enter_context(nc.sbuf_tensor(name, [128, cols], dtype))

    def psum(name, cols, dtype=F32):
        return es.enter_context(nc.psum_tensor(name, [128, cols], dtype))

    s_nw = sb("s_nw", 8); s_cb = sb("s_cb", 8); s_bdt = sb("s_bdt", 8)
    s_D = sb("s_D", 8); s_An = sb("s_An", 128)
    s_idf = sb("s_idf", 128); s_idb = sb("s_idb", 128, BF16)
    s_b1 = sb("s_b1", 64); s_b2 = sb("s_b2", 16)
    s_sel = sb("s_sel", 4 * 128, BF16)
    s_eps = sb("s_eps", 2)
    s_dg = sb("s_dg", 32 * 128, BF16)
    s_wx = sb("s_wx", 8 * 96, BF16)
    s_wdt = sb("s_wdt", DH, BF16)
    s_wi = sb("s_wi", 2 * 1024, BF16)

    s_xg = sb("s_xg", DM)
    s_st = sb("s_st", 32)
    s_rms = sb("s_rms", 16)
    s_xnt = sb("s_xnt", DM, BF16)
    s_rT = sb("s_rT", 128, BF16)
    s_rbc = sb("s_rbc", TC, BF16)
    s_xnfm = sb("s_xnfm", 8 * TC, BF16)
    s_xn2 = sb("s_xn2", 2 * 8 * TC, BF16)
    s_xsc = sb("s_xsc", 2 * (TC + 3), BF16)
    s_hist = sb("s_hist", 24, BF16)
    s_xact = sb("s_xact", 2 * 8 * TC, BF16)
    s_xdm = sb("s_xdm", TC, BF16)

    s_xdf = sb("s_xdf", TC, BF16)
    s_sfs = sb("s_sfs", TC)
    s_del = sb("s_del", 2 * 8 * TC, BF16)
    s_du = sb("s_du", 8 * TC, BF16)
    s_y = sb("s_y", 8 * TC, BF16)
    s_dA = sb("s_dA", 3 * TC, BF16)
    s_X = sb("s_X", 3 * TC, BF16)
    s_pr = s_X
    s_h = sb("s_h", 8 * TC, BF16)
    s_Bb = sb("s_Bb", 4 * TC, BF16)
    s_Cb = sb("s_Cb", 4 * TC, BF16)
    s_car = sb("s_car", 128)
    s_xf = sb("s_xf", 8 * TC, BF16)
    s_opb = sb("s_opb", TC, BF16)
    s_wgm = sb("s_wgm", 2 * 1024, BF16)
    s_wom = sb("s_wom", 2 * 1024, BF16)

    s_rs = sb("s_rs", 2 * TC, BF16)
    s_xfi = sb("s_xfi", 2 * DM)
    s_fst = sb("s_fst", 32)
    s_xnf = sb("s_xnf", DM)
    s_xnff = sb("s_xnff", 8 * TC, BF16)
    s_hff = sb("s_hff", 32 * 256, BF16)
    s_w1m = sb("s_w1m", 4 * 1024, BF16)
    s_w2d = sb("s_w2d", 2 * 2048, BF16)
    s_omf = sb("s_omf", 4 * 256)
    s_otk = sb("s_otk", TC)

    p_f0 = psum("p_f0", TC)
    p_f1 = psum("p_f1", TC)
    p_ft = psum("p_ft", 1024, BF16)
    p_db = psum("p_db", TC)
    p_do = psum("p_do", TC)
    p_bc = psum("p_bc", TC)
    p_n0 = psum("p_n0", TC)
    p_n1 = psum("p_n1", TC)

    blk = es.enter_context(nc.Block())
    OP = em.op
    V = em.v

    free = {k: ("ve", 0) for k in ("f0", "f1", "ft", "db", "do", "bc", "n0", "n1")}
    bf = {}
    PUMP = [lambda k=1: None]
    hi = []
    lo = []
    parked = {}
    rs_issued = set()

    for dst, src in ((s_nw, nw_col), (s_cb, cb_col), (s_bdt, bdt_col),
                     (s_D, D_col), (s_idf, identf), (s_idb, identb),
                     (s_b1, b1_cols), (s_b2, b2_cols)):
        OP("sync", lambda e, d=dst, s=src: e.dma_start(out=d[:], in_=s), inc="ini")
    OP("sync", lambda e: e.dma_start(out=s_An[:], in_=A_lay), inc="ini")
    OP("sync", lambda e: e.dma_start(out=s_sel[0:4, :], in_=sel16[0:4, 0:512]), inc="ini")
    ini_n = 16 * V("ini")

    for m in range(8):
        OP("gpsimd", lambda e, m=m: e.dma_start(
            out=wiscm[m],
            in_=wi_ssm.rearrange("(k p) (m c) -> m p k c", p=128, c=128)[m]),
           inc="gdw")
    OP("gpsimd", lambda e: e.dma_start(
        out=s_wx[:].rearrange("p (k c) -> p k c", c=96),
        in_=w_x_h.rearrange("(k p) c -> p k c", p=128)), inc="gds")
    OP("gpsimd", lambda e: e.dma_start(out=s_wdt[0:64, :], in_=w_dt_h), inc="gds")
    OP("gpsimd", lambda e: e.dma_start(
        out=s_dg[:].rearrange("p (j c) -> p j c", c=128),
        in_=diag.rearrange("j p c -> p j c")), inc="gds")
    gds_n = 16 * V("gds")
    for m in range(8):
        OP("gpsimd", lambda e, m=m: e.dma_start(
            out=wigcm[m],
            in_=wi_gate.rearrange("(k p) (m c) -> m p k c", p=128, c=128)[m]), inc="gdw")
        OP("gpsimd", lambda e, m=m: e.dma_start(
            out=woutcm[m],
            in_=w_out_h.rearrange("(k p) (m c) -> m p k c", p=128, c=128)[m]), inc="gdw")
    for ei, w1_, w2_ in ((0, w1_a, w2_a), (1, w1_b, w2_b)):
        for m in range(32):
            OP("sync" if m % 2 == 0 else "scalar",
               lambda e, ei=ei, m=m, w1_=w1_: e.dma_start(
                   out=w1cm[ei][m],
                   in_=w1_.rearrange("(k p) (m c) -> m p k c", p=128, c=128)[m]),
               inc="gwf%d" % ei)
        for m in range(8):
            OP("sync" if m % 2 == 0 else "scalar",
               lambda e, ei=ei, m=m, w2_=w2_: e.dma_start(
                   out=w2cm[ei][m],
                   in_=w2_.rearrange("(k p) (m c) -> m p k c", p=128, c=128)[m]),
               inc="gwf%d" % ei)
    gdw_n = 16 * V("gdw")
    gwf_n = (16 * V("gwf0"), 16 * V("gwf1"))

    OP("scalar", lambda e: e.activation(s_An[:], s_An[:], AF.Exp, bias=0.0, scale=1.0),
       waits=[("ini", ini_n)], inc="sc")
    OP("vector", lambda e: e.tensor_scalar(s_An[:], s_An[:], -1.0, None, ALU.mult),
       waits=[("sc", V("sc"))], inc="ve")
    aneg_ve = V("ve")
    OP("gpsimd", lambda e: e.memset(s_hist[:], 0.0), inc="gp")
    hist0_gp = V("gp")
    OP("vector", lambda e: e.memset(s_eps[:, 0:1], EPS_LN), inc="ve")
    OP("vector", lambda e: e.memset(s_eps[:, 1:2], EPS_RMS), inc="ve")
    eps_ve = V("ve")

    W_xn2 = [0] * NT
    W_xact = [0] * NT
    W_ar = [0] * NT
    W_rs = [0] * 4
    W_sdone = [dict(pe=0, ve=0, sc=0, gp=0) for _ in range(NT)]
    W_gatepe = [0] * NT
    W_stats = [0] * NT
    W_fstats = [0] * 4
    W_statsc = {}
    W_fstatsc = {}
    W_carn = {}
    W_xactfree = {}
    W_dugp = {}
    W_nloopve = {}
    W_xfve = {}

    def front_gen(t):
        xn2_o = (t % 2) * 8 * TC
        xact_o = (t % 2) * 8 * TC
        sd_gp = W_xactfree.get(t - 2, 0)

        ev = [None]
        for j in range(4):
            r0 = t * TC + j * 128
            OP("sync", lambda e, r0=r0: e.dma_start(out=s_xg[:], in_=x_g[r0:r0 + 128, :]),
               waits=[("ve", bf.get("xg_ve", 0)), ("sc", bf.get("xg_sc", 0))], inc="xg")
            ld = 16 * V("xg")
            OP("vector", lambda e: e.bn_stats(s_st[:, 0:6], s_xg[:, 0:512]),
               waits=[("xg", ld), ("ini", ini_n),
                      ("sc", W_statsc.get(t - 1, 0)), ("ve", V("ve"))], inc="ve")
            OP("vector", lambda e: e.bn_stats(s_st[:, 6:12], s_xg[:, 512:1024]),
               waits=[("ve", V("ve"))], inc="ve")
            OP("vector", lambda e, j=j: e.bn_aggr(
                s_st[:, 16 + 2 * j:18 + 2 * j], s_st[:, 0:12]),
               waits=[("ve", V("ve"))], inc="ve")
            OP("scalar", lambda e, j=j: e.activation(
                s_st[:, 24 + j:25 + j], s_st[:, 17 + 2 * j:18 + 2 * j],
                AF.Ln, bias=s_eps[:, 0:1], scale=1.0),
               waits=[("ve", V("ve")), ("sc", V("sc"))], inc="sc")
            OP("scalar", lambda e, j=j: e.activation(
                s_st[:, 28 + j:29 + j], s_st[:, 24 + j:25 + j],
                AF.Exp, bias=0.0, scale=-0.5),
               waits=[("sc", V("sc"))], inc="sc")
            rstd_sc = V("sc")
            bf["xg_sc"] = rstd_sc
            OP("vector", lambda e, j=j: e.tensor_tensor(
                s_rms[:, 4 + j:5 + j], s_st[:, 28 + j:29 + j],
                s_st[:, 28 + j:29 + j], ALU.mult),
               waits=[("sc", rstd_sc), ("ve", V("ve"))], inc="ve")
            OP("vector", lambda e, j=j: e.tensor_tensor(
                s_rms[:, j:j + 1], s_st[:, 17 + 2 * j:18 + 2 * j],
                s_rms[:, 4 + j:5 + j], ALU.mult),
               waits=[("ve", V("ve"))], inc="ve")
            xo = 0
            OP("vector", lambda e, j=j, xo=xo: e.tensor_scalar(
                s_xnt[:, xo:xo + DM], s_xg[:], s_st[:, 16 + 2 * j:17 + 2 * j],
                s_st[:, 28 + j:29 + j], ALU.subtract, ALU.mult),
               waits=[("pe", bf.get("xnt0", 0)), ("ve", V("ve"))], inc="ve")
            xn_ve = V("ve")
            bf["xg_ve"] = xn_ve
            if ev[0] is not None:
                ev[0]()
            for ck in range(8):
                OP("tensor", lambda e, ck=ck, xo=xo: e.matmul(
                    p_ft[:, ck * 128:(ck + 1) * 128],
                    s_xnt[:, xo + ck * 128:xo + (ck + 1) * 128], s_idb[:],
                    start=True, stop=True, is_transpose=True),
                   waits=([("ve", xn_ve), ("ini", ini_n), free["ft"]]
                          if ck == 0 else ()), inc="pe")
            tr_pe = V("pe")
            bf["xnt0"] = tr_pe

            def mk_ev(j=j, tr_pe=tr_pe):
                def go():
                    OP("vector", lambda e, j=j: e.tensor_copy(
                        s_xnfm.rearrange("p (c s) -> p c s", s=TC)
                        [:, :, j * 128:(j + 1) * 128],
                        p_ft.rearrange("p (c s) -> p c s", s=128)),
                       waits=[("pe", tr_pe)], inc="ve")
                    free["ft"] = ("ve", V("ve"))
                return go
            ev[0] = mk_ev()
            yield
        ev[0]()
        W_statsc[t] = V("sc")
        OP("scalar", lambda e: e.activation(
            s_rms[:, 8:12], s_rms[:, 0:4], AF.Ln, bias=s_eps[:, 1:2], scale=1.0),
           waits=[("ve", V("ve")), ("sc", V("sc"))], inc="sc")
        OP("scalar", lambda e: e.activation(
            s_rms[:, 12:16], s_rms[:, 8:12], AF.Exp, bias=0.0, scale=-0.5),
           waits=[("sc", V("sc"))], inc="sc")
        rmsr_sc = V("sc")
        OP("tensor", lambda e: e.matmul(
            p_bc[0:4, 0:128], s_rms[:, 12:16], s_idf[:], start=True, stop=True,
            is_transpose=True), waits=[("sc", rmsr_sc), free["bc"]], inc="pe")
        OP("vector", lambda e: e.tensor_copy(s_rT[0:4, :], p_bc[0:4, 0:128]),
           waits=[("pe", V("pe"))], inc="ve")
        rT_ve = V("ve")
        for q in range(4):
            OP("tensor", lambda e, q=q: e.matmul(
                p_bc[:, q * 128:(q + 1) * 128], s_sel[0:4, q * 128:(q + 1) * 128],
                s_rT[0:4, 0:128], start=True, stop=True),
               waits=([("ve", rT_ve), ("ve", bf.get("rbc", 0))] if q == 0 else ()),
               inc="pe")
        OP("scalar", lambda e: e.activation(s_rbc[:], p_bc[:], AF.Copy, bias=0.0, scale=1.0),
           waits=[("pe", V("pe"))], inc="sc")
        rbc_sc = V("sc")
        free["bc"] = ("sc", rbc_sc)
        yield
        for ck in range(8):
            OP("vector", lambda e, ck=ck, o=xn2_o: e.scalar_tensor_tensor(
                s_xn2[:, o + ck * TC:o + (ck + 1) * TC],
                s_xnfm[:, ck * TC:(ck + 1) * TC],
                s_nw[:, ck:ck + 1], s_rbc[:], ALU.mult, ALU.mult),
               waits=[("sc", rbc_sc), ("pe", W_gatepe[t - 2] if t >= 2 else 0),
                      ("ve", V("ve"))], inc="ve")
        W_xn2[t] = V("ve")
        W_stats[t] = V("ve")
        bf["rbc"] = W_xn2[t]
        yield
        if debug and t == 0:
            for ck in range(8):
                OP("gpsimd", lambda e, ck=ck: e.dma_start(
                    out=dbg["d_xnfm"][ck], in_=s_xnfm[:, ck * TC:(ck + 1) * TC]),
                   waits=[("ve", W_xn2[t])], inc="gdd")
                OP("gpsimd", lambda e, ck=ck, o=xn2_o: e.dma_start(
                    out=dbg["d_xn2"][ck], in_=s_xn2[:, o + ck * TC:o + (ck + 1) * TC]),
                   waits=[("ve", W_xn2[t])], inc="gdd")

        def wi_load(m):
            ws = m % 2
            OP("sync", lambda e, m=m, ws=ws: e.dma_start(
                out=s_wi[:, ws * 1024:(ws + 1) * 1024], in_=wiscm[m]),
               waits=[("pe", bf.get("wis%d" % ws, 0)), ("gdw", gdw_n)],
               inc="wis%d" % ws)

        def in_proj(m):
            ws = m % 2
            wld = 16 * V("wis%d" % ws)
            pin, kin = (p_f0, "f0") if m % 2 == 0 else (p_f1, "f1")
            if m + 1 < 8:
                wi_load(m + 1)
            for k in range(8):
                OP("tensor", lambda e, m=m, k=k, ws=ws, pin=pin: e.matmul(
                    pin[:], s_wi[:, ws * 1024 + k * 128:ws * 1024 + (k + 1) * 128],
                    s_xn2[:, xn2_o + k * TC:xn2_o + (k + 1) * TC],
                    start=(k == 0), stop=(k == 7)),
                   waits=([("ve", W_xn2[t]), ("wis%d" % ws, wld), free[kin]]
                          if k == 0 else ()),
                   inc=("pe" if k == 7 else None))
            bf["wis%d" % ws] = V("pe")
            return m, V("pe"), pin, kin

        def conv_part(m, apex, pin, kin):
            slot = m % 2
            pco, kco = (p_f1, "f1") if m % 2 == 0 else (p_f0, "f0")
            OP("scalar", lambda e, slot=slot, pin=pin: e.activation(
                s_xsc[:, slot * (TC + 3) + 3:(slot + 1) * (TC + 3)], pin[:], AF.Copy,
                bias=0.0, scale=1.0),
               waits=[("pe", apex), ("pe", bf.get("xsc_pe%d" % slot, 0)),
                      ("sc", V("sc"))], inc="sc")
            ev_sc = V("sc")
            free[kin] = ("sc", ev_sc)
            OP("scalar", lambda e, slot=slot, m=m: e.activation(
                s_xsc[:, slot * (TC + 3):slot * (TC + 3) + 3],
                s_hist[:, m * 3:m * 3 + 3], AF.Copy, bias=0.0, scale=1.0),
               waits=[("gp", hist0_gp), ("sc", V("sc"))], inc="sc")
            OP("scalar", lambda e, slot=slot, m=m: e.activation(
                s_hist[:, m * 3:m * 3 + 3],
                s_xsc[:, slot * (TC + 3) + TC:slot * (TC + 3) + TC + 3],
                AF.Copy, bias=0.0, scale=1.0),
               waits=[("sc", V("sc"))], inc="sc")
            h_sc = V("sc")
            for tap in range(4):
                OP("tensor", lambda e, m=m, tap=tap, slot=slot, pco=pco: e.matmul(
                    pco[:], s_dg[:, (m * 4 + tap) * 128:(m * 4 + tap + 1) * 128],
                    s_xsc[:, slot * (TC + 3) + tap:slot * (TC + 3) + tap + TC],
                    start=(tap == 0), stop=(tap == 3)),
                   waits=([("sc", h_sc), ("gds", gds_n), free[kco]]
                          if tap == 0 else ()),
                   inc=("pe" if tap == 3 else None))
            conv_pe = V("pe")
            OP("scalar", lambda e, m=m, pco=pco, o=xact_o: e.activation(
                s_xact[:, o + m * TC:o + (m + 1) * TC], pco[:], AF.Silu,
                bias=s_cb[:, m:m + 1], scale=1.0),
               waits=[("pe", conv_pe), ("gp", sd_gp)], inc="sc")
            silu_sc = V("sc")
            free[kco] = ("sc", silu_sc)
            bf["xsc_pe%d" % slot] = conv_pe
            OP("tensor", lambda e, m=m, o=xact_o: e.matmul(
                p_db[0:96, :], s_wx[:, m * 96:(m + 1) * 96],
                s_xact[:, o + m * TC:o + (m + 1) * TC],
                start=(m == 0), stop=(m == 7)),
               waits=([("sc", silu_sc), ("gds", gds_n), free["db"]]
                      if m == 0 else [("sc", silu_sc)]),
               inc=("pe" if m == 7 else None))

        wi_load(0)
        prev = None
        for m in range(8):
            if prev is not None:
                conv_part(*prev)
            prev = in_proj(m)
            yield
        conv_part(*prev)
        W_xact[t] = V("sc")
        yield
        front_tail(t)

    def front_tail(t):
        xact_o = (t % 2) * 8 * TC
        OP("vector", lambda e: e.tensor_copy(s_xdm[0:96, :], p_db[0:96, :]),
           waits=[("pe", V("pe")), ("arin", 16 * bf.get("xdm", 0))], inc="ve")
        free["db"] = ("ve", V("ve"))
        OP("sync", lambda e, t=t: e.dma_start(out=ar_i[t], in_=s_xdm[0:96, :]),
           waits=[("ve", V("ve"))], inc="arin")
        bf["xdm"] = V("arin")
        OP("gpsimd", lambda e, t=t: e.collective_compute(
            "AllReduce", ALU.add, replica_groups=GROUPS, ins=[ar_i[t]], outs=[ar_o[t]]),
           waits=[("arin", 16 * V("arin")), ("gdw", gdw_n)], inc="cAR%d" % t)
        W_ar[t] = None
        if debug:
            for m in range(8):
                OP("gpsimd", lambda e, m=m, o=xact_o, t=t: e.dma_start(
                    out=dbg["d_xact"][t, m], in_=s_xact[:, o + m * TC:o + (m + 1) * TC]),
                   waits=[("sc", W_xact[t])], inc="gdd")

    GATESC = {}

    def gates_gen(t):
        xn2_o = (t % 2) * 8 * TC
        gate_sc = [0] * 8
        GATESC[t] = gate_sc
        def wgm_load(m):
            ws = m % 2
            OP("scalar", lambda e, m=m, ws=ws: e.dma_start(
                out=s_wgm[:, ws * 1024:(ws + 1) * 1024], in_=wigcm[m]),
               waits=[("pe", bf.get("wgm%d" % ws, 0)), ("gdw", gdw_n)],
               inc="wgm%d" % ws)

        wgm_load(0)
        psil = [None]
        for m in range(8):
            if psil[0] is not None:
                psil[0]()
                psil[0] = None
            ws = m % 2
            wld = 16 * V("wgm%d" % ws)
            if m + 1 < 8:
                wgm_load(m + 1)
            for k in range(8):
                OP("tensor", lambda e, m=m, k=k, ws=ws, o=xn2_o: e.matmul(
                    p_bc[:], s_wgm[:, ws * 1024 + k * 128:ws * 1024 + (k + 1) * 128],
                    s_xn2[:, o + k * TC:o + (k + 1) * TC],
                    start=(k == 0), stop=(k == 7)),
                   waits=([("wgm%d" % ws, wld), ("ve", W_xn2[t]), free["bc"]]
                          if k == 0 else ()),
                   inc=("pe" if k == 7 else None))
            bf["wgm%d" % ws] = V("pe")
            g_pe = V("pe")

            def mk_sil(m=m, g_pe=g_pe):
                def go():
                    OP("scalar", lambda e, m=m: e.activation(
                        s_xf[:, m * TC:(m + 1) * TC], p_bc[:], AF.Silu,
                        bias=0.0, scale=1.0),
                       waits=[("pe", g_pe), ("rsi", 16 * bf.get("xf_sy", 0)),
                              ("sc", V("sc"))], inc="sc")
                    gate_sc[m] = V("sc")
                    free["bc"] = ("sc", V("sc"))
                return go
            psil[0] = mk_sil()
            yield
        psil[0]()
        W_gatepe[t] = V("pe")

    def outproj_gen(t):
        if debug:
            for m in range(8):
                OP("gpsimd", lambda e, t=t, m=m: e.dma_start(
                    out=dbg["d_xfop"][t, m], in_=s_xf[:, m * TC:(m + 1) * TC]),
                   waits=[("ve", W_xfve[t])], inc="gdd")
        for dm in range(8):
            ws = dm % 2
            OP("sync", lambda e, dm=dm, ws=ws: e.dma_start(
                out=s_wom[:, ws * 1024:(ws + 1) * 1024],
                in_=woutcm[dm]),
               waits=[("pe", bf.get("wom%d" % ws, 0)), ("gdw", gdw_n)],
               inc="wom%d" % ws)
            wld = 16 * V("wom%d" % ws)
            for k in range(8):
                OP("tensor", lambda e, k=k, ws=ws: e.matmul(
                    p_do[:], s_wom[:, ws * 1024 + k * 128:ws * 1024 + (k + 1) * 128],
                    s_xf[:, k * TC:(k + 1) * TC], start=(k == 0), stop=(k == 7)),
                   waits=([("wom%d" % ws, wld), ("ve", W_xfve[t]), free["do"]]
                          if k == 0 else ()),
                   inc=("pe" if k == 7 else None))
            bf["wom%d" % ws] = V("pe")
            OP("scalar", lambda e: e.activation(s_opb[:], p_do[:], AF.Copy, bias=0.0,
                                                scale=1.0),
               waits=[("pe", V("pe")), ("rsi", 16 * bf.get("opb", 0)),
                      ("sc", V("sc"))], inc="sc")
            free["do"] = ("sc", V("sc"))
            OP("scalar", lambda e, t=t, dm=dm: e.dma_start(
                out=rs_i[t // 2][t % 2][dm * 128:(dm + 1) * 128, :], in_=s_opb[:]),
               waits=[("sc", V("sc"))], inc="rsi")
            bf["opb"] = V("rsi")
            yield
        bf["xf_sy"] = V("rsi")
        if t % 2 == 1:
            OP("gpsimd", lambda e, i=t // 2: e.collective_compute(
                "ReduceScatter", ALU.add, replica_groups=GROUPS,
                ins=[rs_i[i]], outs=[rs_o[i]]),
               waits=[("rsi", 16 * V("rsi"))], inc="cRS%d" % (t // 2))
            W_rs[t // 2] = None
            rs_issued.add(t // 2)

    DELDONE = set()
    W_delsc = {}

    def delta_gen(t):
        OP("sync", lambda e, t=t: e.dma_start(
            out=s_xdf[0:64, :], in_=ar_o[t][0:64, :]),
           waits=[("cAR%d" % t, 1), ("pe", bf.get("xdf0", 0))], inc="xdf")
        ld = 16 * V("xdf")
        if debug:
            OP("gpsimd", lambda e, t=t: e.dma_start(
                out=dbg["d_xdbl"][t], in_=ar_o[t]),
               waits=[("xdf", ld)], inc="gdd")
        for m in range(8):
            OP("tensor", lambda e, m=m: e.matmul(
                p_do[:], s_wdt[0:64, m * 128:(m + 1) * 128], s_xdf[0:64, :],
                start=True, stop=True),
               waits=[("xdf", ld), ("gds", gds_n), free["do"]], inc="pe")
            OP("vector", lambda e, m=m: e.tensor_scalar(
                s_sfs[:], p_do[:], s_bdt[:, m:m + 1], SP_CLAMP, ALU.add, ALU.min),
               waits=[("pe", V("pe")), ("sc", bf.get("sfs", 0))], inc="ve")
            free["do"] = ("ve", V("ve"))
            OP("scalar", lambda e, m=m: e.activation(
                s_sfs[:], s_sfs[:], AF.Exp, bias=s_bdt[:, m:m + 1], scale=1.0),
               waits=[("ve", V("ve"))], inc="sc")
            OP("scalar", lambda e, m=m, do=(t % 2) * 8 * TC: e.activation(
                s_del[:, do + m * TC:do + (m + 1) * TC], s_sfs[:], AF.Ln,
                bias=1.0, scale=1.0),
               waits=[("gp", W_dugp.get(t - 2, 0)), ("sc", V("sc"))], inc="sc")
            bf["sfs"] = V("sc")
            if debug:
                OP("gpsimd", lambda e, t=t, m=m, do=(t % 2) * 8 * TC: e.dma_start(
                    out=dbg["d_delta"][t, m],
                    in_=s_del[:, do + m * TC:do + (m + 1) * TC]),
                   waits=[("sc", V("sc"))], inc="gdd")
            if m % 2 == 1:
                yield
        W_delsc[t] = V("sc")
        bf["xdf0"] = V("pe")
        DELDONE.add(t)

    def scan(t):
        xact_o = (t % 2) * 8 * TC

        if t not in DELDONE:
            for _ in delta_gen(t):
                pass
        for m in range(8):
            OP("gpsimd", lambda e, m=m, o=xact_o, do=(t % 2) * 8 * TC:
               e.tensor_tensor(
                   s_du[:, m * TC:(m + 1) * TC],
                   s_del[:, do + m * TC:do + (m + 1) * TC],
                   s_xact[:, o + m * TC:o + (m + 1) * TC], ALU.mult),
               waits=[("sc", W_delsc[t]), ("sc", W_xact[t]),
                      ("ve", W_nloopve.get(t - 1, 0))], inc="gp")
            OP("scalar", lambda e, m=m, o=xact_o: e.activation(
                s_y[:, m * TC:(m + 1) * TC], s_xact[:, o + m * TC:o + (m + 1) * TC],
                AF.Copy, bias=0.0, scale=s_D[:, m:m + 1]),
               waits=[("ve", W_xfve.get(t - 1, 0))], inc="sc")
        du_gp = V("gp")
        yinit_sc = V("sc")
        W_dugp[t] = du_gp
        W_xactfree[t] = du_gp

        if t >= 1:
            hi.append(outproj_gen(t - 1))
        hi.append(gates_gen(t))
        if t + 1 < NT:
            hi.append(delta_gen(t + 1))
        if t + 2 < NT:
            hi.append(front_gen(t + 2))

        def bc_load(n):
            bs = n % 4
            OP("sync", lambda e, t=t, n=n, bs=bs: e.dma_start(
                out=s_Bb[:, bs * TC:(bs + 1) * TC],
                in_=ar_o[t][64 + n:65 + n, :].partition_broadcast(128)),
               waits=[("cAR%d" % t, 1), ("ve", bf.get("Bb%d" % bs, 0))],
               inc="bB%d" % bs)
            OP("sync", lambda e, t=t, n=n, bs=bs: e.dma_start(
                out=s_Cb[:, bs * TC:(bs + 1) * TC],
                in_=ar_o[t][80 + n:81 + n, :].partition_broadcast(128)),
               waits=[("ve", bf.get("Cb%d" % bs, 0)),
                      ("gp", bf.get("Cbgp%d" % bs, 0))], inc="bC%d" % bs)

        for n in range(3):
            bc_load(n)
        for n in range(NS):
            if n + 3 < NS:
                bc_load(n + 3)
            bs = n % 4
            bcB_n = 16 * V("bB%d" % bs)
            bcC_n = 16 * V("bC%d" % bs)
            for m in range(8):
                a = (n * 8 + m) % 3
                OP("scalar", lambda e, m=m, n=n, a=a, do=(t % 2) * 8 * TC:
                   e.activation(
                       s_dA[:, a * TC:(a + 1) * TC],
                       s_del[:, do + m * TC:do + (m + 1) * TC], AF.Exp,
                       bias=0.0, scale=s_An[:, m * 16 + n:m * 16 + n + 1]),
                   waits=[("ve", max(bf.get("dA%d" % a, 0), aneg_ve))], inc="sc")
                dA_sc = V("sc")
                OP("vector", lambda e, m=m, a=a, bs=bs: e.tensor_tensor(
                    s_X[:, a * TC:(a + 1) * TC], s_du[:, m * TC:(m + 1) * TC],
                    s_Bb[:, bs * TC:(bs + 1) * TC], ALU.mult),
                   waits=[("gp", du_gp), ("gp", bf.get("pr_gp%d" % a, 0)),
                          ("bB%d" % bs, bcB_n)], inc="ve")
                X_ve = V("ve")
                init = 0.0 if t == 0 else s_car[:, m * 16 + n:m * 16 + n + 1]
                OP("vector", lambda e, m=m, a=a, init=init: e.tensor_tensor_scan(
                    s_h[:, m * TC:(m + 1) * TC], s_dA[:, a * TC:(a + 1) * TC],
                    s_X[:, a * TC:(a + 1) * TC], init, ALU.mult, ALU.add),
                   waits=[("sc", dA_sc), ("gp", max(bf.get("hgp%d" % m, 0),
                                                     W_carn.get((t - 1, n), 0),
                                                     W_carn.get((t, n - 1), 0))),
                          ("ve", X_ve)], inc="ve")
                scan_ve = V("ve")
                bf["dA%d" % a] = scan_ve
                if debug and n == 0:
                    OP("gpsimd", lambda e, t=t, m=m: e.dma_start(
                        out=dbg["d_h0"][t, m], in_=s_h[:, m * TC:(m + 1) * TC]),
                       waits=[("ve", scan_ve)], inc="gdd")
                OP("gpsimd", lambda e, m=m, a=a, bs=bs: e.tensor_tensor(
                    s_pr[:, a * TC:(a + 1) * TC], s_Cb[:, bs * TC:(bs + 1) * TC],
                    s_h[:, m * TC:(m + 1) * TC], ALU.mult),
                   waits=[("gp", bf.get("pr_gp%d" % a, 0)), ("ve", V("ve")),
                          ("bC%d" % bs, bcC_n)], inc="gp")
                pr_ve = V("ve")
                OP("gpsimd", lambda e, m=m, a=a: e.tensor_tensor(
                    s_y[:, m * TC:(m + 1) * TC], s_y[:, m * TC:(m + 1) * TC],
                    s_pr[:, a * TC:(a + 1) * TC], ALU.add),
                   waits=[("ve", pr_ve), ("sc", yinit_sc), ("gp", V("gp"))],
                   inc="gp")
                bf["pr_gp%d" % a] = V("gp")
                bf["hgp%d" % m] = V("gp")
            OP("gpsimd", lambda e, n=n: e.tensor_copy(
                s_car.rearrange("p (m x) -> p m x", x=16)[:, :, n:n + 1],
                s_h.rearrange("p (m c) -> p m c", c=TC)[:, :, TC - 1:TC]),
               waits=[("gp", V("gp")), ("ve", V("ve"))], inc="gp")
            W_carn[(t, n)] = V("gp")
            bf["Bb%d" % bs] = V("ve")
            bf["Cb%d" % bs] = V("ve")
            bf["Cbgp%d" % bs] = V("gp")
            if n >= 5:
                PUMP[0](1, lo_max=0)
            if n >= 10:
                PUMP[0](1, lo_max=0)
            PUMP[0](4)
        ya_gp = V("gp")
        W_nloopve[t] = V("ve")
        while hi:
            try:
                next(hi[0])
            except StopIteration:
                hi.pop(0)
        if debug:
            for m in range(8):
                OP("gpsimd", lambda e, t=t, m=m: e.dma_start(
                    out=dbg["d_y"][t, m], in_=s_y[:, m * TC:(m + 1) * TC]),
                   waits=[("gp", ya_gp)], inc="gdd")

        for m in range(8):
            OP("vector", lambda e, m=m: e.tensor_tensor(
                s_xf[:, m * TC:(m + 1) * TC], s_y[:, m * TC:(m + 1) * TC],
                s_xf[:, m * TC:(m + 1) * TC], ALU.mult),
               waits=[("sc", GATESC[t][m]), ("gp", bf["hgp%d" % m]),
                      ("ve", V("ve"))], inc="ve")
        xf_ve = V("ve")
        W_xfve[t] = xf_ve
        if debug:
            for m in range(8):
                OP("gpsimd", lambda e, t=t, m=m: e.dma_start(
                    out=dbg["d_xf"][t, m], in_=s_xf[:, m * TC:(m + 1) * TC]),
                   waits=[("ve", xf_ve)], inc="gdd")

    def fin_gen(i):
        eloc = 0 if i < 2 else 1
        fev = [None]
        for j in range(4):
            r0 = i * TC + j * 128
            xo = (j % 2) * DM
            OP("scalar", lambda e, r0=r0, xo=xo: e.dma_start(
                out=s_xfi[:, xo:xo + DM], in_=x_fin[r0:r0 + 128, :]),
               waits=[("ve", bf.get("xfi%d" % (j % 2), 0)),
                      ("sc", bf.get("xfi_sc%d" % (j % 2), 0))], inc="xfi%d" % (j % 2))
            ld = 16 * V("xfi%d" % (j % 2))
            OP("vector", lambda e, xo=xo: e.bn_stats(
                s_fst[:, 0:6], s_xfi[:, xo:xo + 512]),
               waits=[("xfi%d" % (j % 2), ld),
                      ("sc", W_fstatsc.get(i - 1, 0)), ("ve", V("ve"))], inc="ve")
            OP("vector", lambda e, xo=xo: e.bn_stats(
                s_fst[:, 6:12], s_xfi[:, xo + 512:xo + 1024]),
               waits=[("ve", V("ve"))], inc="ve")
            OP("vector", lambda e, j=j: e.bn_aggr(
                s_fst[:, 16 + 2 * j:18 + 2 * j], s_fst[:, 0:12]),
               waits=[("ve", V("ve"))], inc="ve")
            OP("scalar", lambda e, j=j: e.activation(
                s_fst[:, 24 + j:25 + j], s_fst[:, 17 + 2 * j:18 + 2 * j],
                AF.Ln, bias=s_eps[:, 0:1], scale=1.0),
               waits=[("ve", V("ve")), ("sc", V("sc"))], inc="sc")
            OP("scalar", lambda e, j=j: e.activation(
                s_fst[:, 17 + 2 * j:18 + 2 * j], s_fst[:, 24 + j:25 + j],
                AF.Exp, bias=0.0, scale=-0.5),
               waits=[("sc", V("sc"))], inc="sc")
            OP("vector", lambda e, j=j, xo=xo: e.tensor_scalar(
                s_xnf[:], s_xfi[:, xo:xo + DM], s_fst[:, 16 + 2 * j:17 + 2 * j],
                s_fst[:, 17 + 2 * j:18 + 2 * j], ALU.subtract, ALU.mult),
               waits=[("pe", bf.get("xnf", 0)), ("sc", V("sc")), ("ve", V("ve"))],
               inc="ve")
            xnf_ve = V("ve")
            bf["xfi%d" % (j % 2)] = xnf_ve
            if fev[0] is not None:
                fev[0]()
            trs = []
            for ckh in range(2):
                pb, kb = (p_n0, "n0") if ckh == 0 else (p_n1, "n1")
                for q in range(4):
                    ck = ckh * 4 + q
                    OP("tensor", lambda e, ck=ck, q=q, pb=pb: e.matmul(
                        pb[:, q * 128:(q + 1) * 128], s_xnf[:, ck * 128:(ck + 1) * 128],
                        s_idf[:], start=True, stop=True, is_transpose=True),
                       waits=([("ve", xnf_ve), free[kb]] if q == 0 else ()), inc="pe")
                trs.append((ckh, pb, kb, V("pe")))
            bf["xnf"] = V("pe")

            def mk_fev(j=j, trs=trs):
                def go():
                    for ckh, pb, kb, tr_pe in trs:
                        OP("vector", lambda e, j=j, ckh=ckh, pb=pb: e.tensor_copy(
                            s_xnff.rearrange("p (c s) -> p c s", s=TC)
                            [:, ckh * 4:(ckh + 1) * 4, j * 128:(j + 1) * 128],
                            pb.rearrange("p (c s) -> p c s", s=128)[:, 0:4, :]),
                           waits=[("pe", tr_pe), ("pe", bf.get("xnff", 0))], inc="ve")
                        free[kb] = ("ve", V("ve"))
                return go
            fev[0] = mk_fev()
            yield
        fev[0]()
        W_fstatsc[i] = V("sc")
        xnff_ve = V("ve")
        W_fstats[i] = V("ve")
        yield

        for sub in range(2):
            gel = [None]
            for m in range(32):
                if gel[0] is not None:
                    gel[0]()
                ws = m % 4
                OP("sync" if ws % 2 == 0 else "scalar",
                   lambda e, m=m, ws=ws, eloc=eloc: e.dma_start(
                       out=s_w1m[:, ws * 1024:(ws + 1) * 1024],
                       in_=w1cm[eloc][m]),
                   waits=[("pe", bf.get("w1m%d" % ws, 0)),
                          ("gwf%d" % eloc, gwf_n[eloc])],
                   inc="w1m%d" % ws)
                wld = 16 * V("w1m%d" % ws)
                pb, kb = (p_n0, "n0") if m % 2 == 0 else (p_n1, "n1")
                for k in range(8):
                    OP("tensor", lambda e, m=m, k=k, ws=ws, sub=sub, pb=pb: e.matmul(
                        pb[:, 0:256],
                        s_w1m[:, ws * 1024 + k * 128:ws * 1024 + (k + 1) * 128],
                        s_xnff.rearrange("p (c s) -> p c s", s=TC)
                        [:, k, sub * 256:(sub + 1) * 256],
                        start=(k == 0), stop=(k == 7)),
                       waits=([("w1m%d" % ws, wld), ("ve", xnff_ve), free[kb]]
                              if k == 0 else ()),
                       inc=("pe" if k == 7 else None))
                bf["w1m%d" % ws] = V("pe")
                w1_pe = V("pe")

                def mk_gel(m=m, pb=pb, kb=kb, w1_pe=w1_pe):
                    def go():
                        OP("scalar", lambda e, m=m, eloc=eloc, pb=pb: e.activation(
                            s_hff[:, m * 256:(m + 1) * 256], pb[:, 0:256],
                            AF.Gelu_apprx_tanh,
                            bias=s_b1[:, eloc * 32 + m:eloc * 32 + m + 1], scale=1.0),
                           waits=[("pe", w1_pe),
                                  ("pe", bf.get("hff", 0) if m == 0 else 0)],
                           inc="sc")
                        free[kb] = ("sc", V("sc"))
                    return go
                gel[0] = mk_gel()
                if m % 2 == 1:
                    yield
            gel[0]()
            hff_sc = V("sc")
            if debug and i == 0 and sub == 0:
                for m in range(8):
                    OP("gpsimd", lambda e, m=m: e.dma_start(
                        out=dbg["d_hff"][m], in_=s_hff[:, m * 256:(m + 1) * 256]),
                       waits=[("sc", hff_sc)], inc="gdd")
            if sub == 0:
                while i not in rs_issued:
                    yield "blk"
                if debug:
                    OP("gpsimd", lambda e, i=i: e.dma_start(
                        out=dbg["d_rso"][i], in_=rs_o[i]),
                       waits=[("cRS%d" % i, 1)], inc="gdd")
            for dmh in range(2):
                pomf = []
                for q in range(4):
                    if len(pomf) >= 2:
                        pomf.pop(0)()
                    dm = dmh * 4 + q
                    ws = dm % 2
                    for hf in range(2):
                        OP("sync" if hf == 0 else "scalar",
                           lambda e, dm=dm, hf=hf, eloc=eloc: e.dma_start(
                               out=s_w2d[:, hf * 2048:(hf + 1) * 2048],
                               in_=w2cm[eloc][dm][:, hf * 16:(hf + 1) * 16, :]),
                           waits=[("pe", bf.get("w2d%d" % hf, 0)),
                                  ("gwf%d" % eloc, gwf_n[eloc])],
                           inc="w2d%d" % hf)
                    wld = (16 * V("w2d0"), 16 * V("w2d1"))
                    pb, kb = (p_n0, "n0") if dm % 2 == 0 else (p_n1, "n1")
                    OP("scalar", lambda e, i=i, dm=dm: e.dma_start(
                        out=s_rs[:, (dm % 2) * TC:(dm % 2 + 1) * TC],
                        in_=rs_o[i][dm * 128:(dm + 1) * 128, :]),
                       waits=[("cRS%d" % i, 1), ("pe", bf.get("rs%d" % (dm % 2), 0))],
                       inc="rso%d" % (dm % 2))
                    rld = 16 * V("rso%d" % (dm % 2))
                    for k in range(32):
                        hf = k // 16
                        OP("tensor", lambda e, k=k, hf=hf, sub=sub, pb=pb: e.matmul(
                            pb[:, 0:256],
                            s_w2d[:, hf * 2048 + (k % 16) * 128:hf * 2048 + (k % 16 + 1) * 128],
                            s_hff[:, k * 256:(k + 1) * 256],
                            start=(k == 0), stop=False),
                           waits=([("w2d0", wld[0]), ("sc", hff_sc), free[kb]]
                                  if k == 0 else
                                  ([("w2d1", wld[1])] if k == 16 else ())),
                           inc=None)
                    OP("tensor", lambda e, dm=dm, sub=sub, pb=pb: e.matmul(
                        pb[:, 0:256], s_idb[:],
                        s_rs[:, (dm % 2) * TC + sub * 256:(dm % 2) * TC + (sub + 1) * 256],
                        start=False, stop=False),
                       waits=[("rso%d" % (dm % 2), rld)], inc=None)
                    OP("tensor", lambda e, dm=dm, sub=sub, pb=pb: e.matmul(
                        pb[:, 0:256], s_idb[:],
                        s_xnff.rearrange("p (c s) -> p c s", s=TC)
                        [:, dm, sub * 256:(sub + 1) * 256],
                        start=False, stop=True), inc="pe")
                    bf["w2d0"] = V("pe"); bf["w2d1"] = V("pe")
                    bf["rs%d" % (dm % 2)] = V("pe")
                    w2_pe = V("pe")

                    def mk_omf(q=q, dm=dm, pb=pb, kb=kb, w2_pe=w2_pe):
                        def go():
                            OP("vector", lambda e, dm=dm, q=q, eloc=eloc, pb=pb:
                               e.tensor_scalar(
                                   s_omf[:, q * 256:(q + 1) * 256], pb[:, 0:256],
                                   s_b2[:, eloc * 8 + dm:eloc * 8 + dm + 1], None,
                                   ALU.add),
                               waits=[("pe", w2_pe),
                                      ("pe", bf.get("omf", 0) if q == 0 else 0)],
                               inc="ve")
                            free[kb] = ("ve", V("ve"))
                        return go
                    pomf.append(mk_omf())
                    yield
                for f in pomf:
                    f()
                pomf = []
                omf_ve = V("ve")

                def mk_xadd(jj):
                    j = sub * 2 + jj
                    xo = (j % 2) * DM
                    pb, kb = (p_n0, "n0") if jj == 0 else (p_n1, "n1")
                    OP("scalar", lambda e, j=j, xo=xo, i=i: e.dma_start(
                        out=s_xfi[:, xo:xo + DM],
                        in_=x_fin[i * TC + j * 128:i * TC + (j + 1) * 128, :]),
                       waits=[("ve", bf.get("xfi%d" % (j % 2), 0)),
                              ("sc", bf.get("xfi_sc%d" % (j % 2), 0))],
                       inc="xfi%d" % (j % 2))
                    xld = 16 * V("xfi%d" % (j % 2))
                    for q in range(4):
                        OP("tensor", lambda e, q=q, jj=jj, pb=pb: e.matmul(
                            pb[:, q * 128:(q + 1) * 128],
                            s_omf[:, q * 256 + jj * 128:q * 256 + (jj + 1) * 128],
                            s_idf[:], start=True, stop=True, is_transpose=True),
                           waits=([("ve", omf_ve), free[kb]] if q == 0 else ()),
                           inc="pe")
                    tr_pe = V("pe")
                    bf["omf"] = tr_pe

                    def go(j=j, xo=xo, pb=pb, kb=kb, tr_pe=tr_pe, xld=xld,
                           dmh=dmh, i=i):
                        OP("vector", lambda e, xo=xo, dmh=dmh, pb=pb: e.tensor_tensor(
                            s_otk[:, 0:TC],
                            s_xfi[:, xo + dmh * TC:xo + (dmh + 1) * TC], pb[:],
                            ALU.add),
                           waits=[("pe", tr_pe), ("xfi%d" % (j % 2), xld),
                                  ("out", 16 * bf.get("otk0", 0))], inc="ve")
                        free[kb] = ("ve", V("ve"))
                        bf["xfi%d" % (j % 2)] = V("ve")
                        OP("scalar", lambda e, i=i, j=j, dmh=dmh: e.dma_start(
                            out=out[i * TC + j * 128:i * TC + (j + 1) * 128,
                                    dmh * TC:(dmh + 1) * TC],
                            in_=s_otk[:, 0:TC]),
                           waits=[("ve", V("ve"))], inc="out")
                        bf["otk0"] = V("out")
                    return go
                xa = mk_xadd(0)
                yield
                xa()
                xa = mk_xadd(1)
                yield
                xa()
            bf["hff"] = V("pe")
        bf["xnff"] = V("pe")

    def pump(k=2, lo_max=2, hi_max=2):
        lo_n = 0
        hi_n = 0
        last_gen = [None]
        for _ in range(k):
            while hi and hi_n < hi_max and hi[0] is not last_gen[0]:
                try:
                    g = hi[0]
                    next(g)
                    last_gen[0] = g
                    hi_n += 1
                    break
                except StopIteration:
                    hi.pop(0)
            else:
                if lo_n >= lo_max:
                    return
                while lo:
                    try:
                        r = next(lo[0])
                    except StopIteration:
                        lo.pop(0)
                        continue
                    if isinstance(r, str):
                        return
                    lo_n += 1
                    break
                else:
                    return

    def front_full(t):
        for _ in front_gen(t):
            pass

    PUMP[0] = pump
    front_full(0)
    front_full(1)
    lo.append(fin_gen(0))
    scan(0)
    scan(1)
    lo.append(fin_gen(1))
    scan(2)
    scan(3)
    lo.append(fin_gen(2))
    scan(4)
    scan(5)
    lo.append(fin_gen(3))
    scan(6)
    scan(7)
    hi.append(outproj_gen(7))
    while hi:
        try:
            next(hi[0])
        except StopIteration:
            hi.pop(0)
    while lo:
        try:
            r = next(lo[0])
        except StopIteration:
            lo.pop(0)
            continue
        if isinstance(r, str):
            raise RuntimeError("fin blocked after all RS issued")

    end_waits = [("out", 16 * V("out"))]
    if debug:
        end_waits.append(("gdd", 16 * V("gdd")))
    OP("gpsimd", lambda e: e.nop(), waits=end_waits)

    em.finalize(blk)
    es.close()
    return nc, dbg



def _prep_inputs(inputs):
    x = np.asarray(inputs["L0_attn_x"])
    norm_w = np.asarray(inputs["L0_attn_norm_w"])
    w_in = np.asarray(inputs["L0_attn_w_in"])
    conv_w = np.asarray(inputs["L0_attn_conv_w"])
    conv_b = np.asarray(inputs["L0_attn_conv_b"])
    w_x = np.asarray(inputs["L0_attn_w_x"])
    w_dt = np.asarray(inputs["L0_attn_w_dt"])
    b_dt = np.asarray(inputs["L0_attn_b_dt"])
    A_log = np.asarray(inputs["L0_attn_A_log"])
    D = np.asarray(inputs["L0_attn_D"])
    w_out = np.asarray(inputs["L0_attn_w_out"])
    w1 = np.asarray(inputs["L0_ffn_w1"])
    b1 = np.asarray(inputs["L0_ffn_b1"])
    w2 = np.asarray(inputs["L0_ffn_w2"])
    b2 = np.asarray(inputs["L0_ffn_b2"])

    identf = np.eye(128, dtype=np.float32)
    identb = np.eye(128).astype(ml_dtypes.bfloat16)
    sel16 = np.kron(np.eye(16), np.ones((1, 128))).astype(ml_dtypes.bfloat16)

    def cols(v):
        return np.ascontiguousarray(v.reshape(8, 128).T.astype(np.float32))

    in_maps = []
    for c in range(8):
        b, g = c // 2, c % 2
        lo = g * DH
        xg = np.ascontiguousarray(x[b].astype(np.float32))
        slabs = [xg[(2 * i + g) * TC:(2 * i + g + 1) * TC] for i in range(4)]
        x_fin_ = np.ascontiguousarray(np.concatenate(slabs, axis=0))
        dg = np.zeros((32, 128, 128), np.float32)
        cw = conv_w[lo:lo + DH, 0, :]
        for m in range(8):
            for tap in range(4):
                np.fill_diagonal(dg[m * 4 + tap], cw[m * 128:(m + 1) * 128, tap])
        Al = A_log[lo:lo + DH].reshape(8, 128, NS).transpose(1, 0, 2).reshape(128, 128)
        e0, e1 = 2 * b, 2 * b + 1
        in_maps.append({
            "x_g": xg,
            "x_fin": x_fin_,
            "wi_ssm": np.ascontiguousarray(w_in[:, lo:lo + DH]),
            "wi_gate": np.ascontiguousarray(w_in[:, 2048 + lo:2048 + lo + DH]),
            "w_x_h": np.ascontiguousarray(w_x[lo:lo + DH, :]),
            "w_dt_h": np.ascontiguousarray(w_dt[:, lo:lo + DH]),
            "w_out_h": np.ascontiguousarray(w_out[lo:lo + DH, :]),
            "diag": dg,
            "cb_col": cols(conv_b[lo:lo + DH]),
            "bdt_col": cols(b_dt[lo:lo + DH]),
            "D_col": cols(D[lo:lo + DH]),
            "nw_col": cols(norm_w),
            "A_lay": np.ascontiguousarray(Al.astype(np.float32)),
            "w1_a": np.ascontiguousarray(w1[e0].astype(ml_dtypes.bfloat16)),
            "w1_b": np.ascontiguousarray(w1[e1].astype(ml_dtypes.bfloat16)),
            "w2_a": np.ascontiguousarray(w2[e0].astype(ml_dtypes.bfloat16)),
            "w2_b": np.ascontiguousarray(w2[e1].astype(ml_dtypes.bfloat16)),
            "b1_cols": np.ascontiguousarray(np.concatenate(
                [b1[e0].reshape(32, 128).T, b1[e1].reshape(32, 128).T], axis=1
            ).astype(np.float32)),
            "b2_cols": np.ascontiguousarray(np.concatenate(
                [b2[e0].reshape(8, 128).T, b2[e1].reshape(8, 128).T], axis=1
            ).astype(np.float32)),
            "identf": identf, "identb": identb,
            "sel16": np.ascontiguousarray(sel16),
        })
    return in_maps


_NC_CACHE = {}


def kernel(**inputs):
    in_maps = _prep_inputs(inputs)
    if "nc" not in _NC_CACHE:
        _NC_CACHE["nc"], _ = build_nc(debug=False)
    nc = _NC_CACHE["nc"]
    res = run_bass_kernel_spmd(nc, in_maps, core_ids=list(range(8)))
    o = np.empty((4, L, DM), np.float32)
    for c in range(8):
        b, g = c // 2, c % 2
        r = res.results[c]["o"]
        for i in range(4):
            o[b, (2 * i + g) * TC:(2 * i + g + 1) * TC] = r[i * TC:(i + 1) * TC]
    return o
